# version 05
# baseline (speedup 1.0000x reference)
import contextlib
import math
import numpy as np
import concourse.bass as bass
import concourse.mybir as mybir
from concourse.bass_utils import run_bass_kernel_spmd

F32 = mybir.dt.float32
BF16 = mybir.dt.bfloat16
AF = mybir.ActivationFunctionType
ALU = mybir.AluOpType
AX = mybir.AxisListType

ENGINES = ("sync", "scalar", "vector", "tensor", "gpsimd")
DMA_QUEUES = ("sync", "scalar", "gpsimd")

CFG_FULL = dict(D=2048, B=4, S=4096, L0_HEADS=32, L0_HD=64, MAX_POS=512,
                FFN_H=5632, L1_H=16, L1_KVH=4, L1_HD=128, INTER=1024, E=8,
                BASE=10000.0, RMS_EPS=1.1920929e-07, LN_EPS=1e-5)


class Seq:

    def __init__(self, nc):
        self.nc = nc
        self._stack = contextlib.ExitStack()
        self.esem = {}
        self.chsem = {}
        for e in ENGINES:
            self.esem[e] = [self._stack.enter_context(nc.semaphore("se_" + e)), 0]
        self.last_writer = {}
        self.readers = {}
        self.steps = {e: [] for e in ENGINES}
        self.n_steps = 0

    def _ch(self, ch):
        if ch not in self.chsem:
            self.chsem[ch] = [self._stack.enter_context(self.nc.semaphore("ch_" + ch)), 0]
        return self.chsem[ch]

    def step(self, engine, fn, reads=(), writes=(), dma=False, n_dma=1, ch=None):
        waits = {}

        def need(sv):
            if sv is None:
                return
            h, v = sv
            k = id(h)
            if k not in waits or waits[k][1] < v:
                waits[k] = (h, v)

        for t in reads:
            need(self.last_writer.get(t))
        for t in writes:
            need(self.last_writer.get(t))
            for sv in self.readers.get(t, ()):
                need(sv)

        if dma:
            s = self._ch(engine + ":" + ch)
            s[1] += 16 * n_dma
        else:
            s = self.esem[engine]
            s[1] += 1
        marker = (s[0], s[1])
        self.steps[engine].append((fn, list(waits.values()), s[0], dma))
        self.n_steps += 1
        for t in writes:
            self.last_writer[t] = marker
            self.readers[t] = []
        for t in reads:
            self.readers.setdefault(t, []).append(marker)
        return marker

    def emit(self, block):
        for engine in ENGINES:
            steps = self.steps[engine]
            if not steps:
                continue

            @getattr(block, engine)
            def _(eng, steps=steps, engine=engine):
                seen = {}
                for fn, waits, sem, is_dma in steps:
                    for h, v in waits:
                        k = id(h)
                        if seen.get(k, -1) >= v:
                            continue
                        eng.wait_ge(h, v)
                        seen[k] = v
                    if is_dma:
                        fn(eng, lambda inst: inst.then_inc(sem, 16))
                    else:
                        inst = fn(eng)
                        inst.then_inc(sem, 1)

    def fence(self):
        targets = [(s[0], s[1]) for s in self.chsem.values() if s[1] > 0]
        esnap = {e: (s[0], s[1]) for e, s in self.esem.items()}
        for e in ENGINES:
            others = [esnap[o] for o in ENGINES if o != e and esnap[o][1] > 0]

            def fn(eng, others=others, targets=targets):
                eng.drain()
                for h, v in targets:
                    eng.wait_ge(h, v)
                for h, v in others:
                    eng.wait_ge(h, v)
                return eng.drain()
            self.step(e, fn)

    def add_final_wait(self):
        targets = [(q, s[0], s[1]) for q, s in self.chsem.items() if s[1] > 0]
        esems = [(e, self.esem[e][0], self.esem[e][1]) for e in ENGINES
                 if self.esem[e][1] > 0]

        def fn(eng):
            for _, h, v in targets:
                eng.wait_ge(h, v)
            for name, h, v in esems:
                if name != "sync":
                    eng.wait_ge(h, v)
            return eng.drain()

        s = self.esem["sync"]
        s[1] += 1
        self.steps["sync"].append((fn, [], s[0], False))


def _dj_view(ap, t0, tc):
    return ap.rearrange("(j p) t -> p j t", p=128)[:, :, t0:t0 + tc]


def load_cast(sq, queue, dst_ap, src_ap, reads, writes, ch):
    def fn(e, mark):
        mark(e.dma_start(dst_ap, src_ap))
    sq.step(queue, fn, reads=reads, writes=writes, dma=True, n_dma=1, ch=ch)


def store(sq, queue, dst_ap, src_ap, reads, writes, ch):
    def fn(e, mark):
        mark(e.dma_start(dst_ap, src_ap))
    sq.step(queue, fn, reads=reads, writes=writes, dma=True, n_dma=1, ch=ch)


def build_launch1(cfg):
    D, T = cfg["D"], cfg["T"]
    TC = min(512, T)
    DJ, NT = D // 128, T // TC
    EPS = cfg["LN_EPS"]
    nc = bass.Bass("TRN2", target_bir_lowering=False)
    xT = nc.dram_tensor("xT", [D, T], F32, kind="ExternalInput").ap()
    wqT = nc.dram_tensor("wqT", [D, D], F32, kind="ExternalInput").ap()
    wkT = nc.dram_tensor("wkT", [D, D], F32, kind="ExternalInput").ap()
    wvT = nc.dram_tensor("wvT", [D, D], F32, kind="ExternalInput").ap()
    xnT = nc.dram_tensor("xnT", [D, T], BF16, kind="ExternalOutput").ap()
    qT = nc.dram_tensor("qT", [D, T], BF16, kind="ExternalOutput").ap()
    kT = nc.dram_tensor("kT", [D, T], BF16, kind="ExternalOutput").ap()
    V = nc.dram_tensor("V", [T, D], BF16, kind="ExternalOutput").ap()

    sq = Seq(nc)
    with contextlib.ExitStack() as es:
        sb = lambda name, shape, dt: es.enter_context(nc.sbuf_tensor(name, shape, dt))
        ps = lambda name, shape, dt=F32: es.enter_context(nc.psum_tensor(name, shape, dt))
        ones_bf = sb("ones_bf", [128, 1], BF16)
        ones_f32r = sb("ones_f32r", [1, 128], F32)
        xc = sb("xc", [128, DJ, TC], F32)
        xcb = sb("xcb", [128, 2, DJ, TC], BF16)
        xsqb = sb("xsqb", [128, DJ, TC], BF16)
        xnb = sb("xnb", [128, 2, DJ, TC], BF16)
        mrow = sb("mrow", [1, TC], F32)
        rrow = sb("rrow", [1, TC], F32)
        w_sb = sb("w_sb", [128, DJ, D], BF16)
        ob = sb("ob", [128, 4, 512], BF16)
        ps_sum = ps("ps_sum", [1, 512])
        ps_sumsq = ps("ps_sumsq", [1, 512])
        ps_mbc = ps("ps_mbc", [128, 512])
        ps_rbc = ps("ps_rbc", [128, 512])
        ps_o = ps("ps_o", [128, 4, 512])

        sq.step("gpsimd", lambda e: e.memset(ones_bf[:], 1.0), writes=[("ones_bf",)])
        sq.step("gpsimd", lambda e: e.memset(ones_f32r[:], 1.0), writes=[("ones_f32r",)])

        for it in range(NT):
            t0 = it * TC
            sl = it % 2
            load_cast(sq, "sync", xc[:], _dj_view(xT, t0, TC),
                      reads=[], writes=[("xc",)], ch="xc")
            load_cast(sq, "gpsimd", xcb[:, sl], _dj_view(xT, t0, TC),
                      reads=[], writes=[("xcb", sl)], ch="xcb%d" % sl)
            sq.step("scalar", lambda e, sl=sl: e.activation(xsqb[:], xcb[:, sl], AF.Square),
                    reads=[("xcb", sl)], writes=[("xsqb",)])

            def mm_sums(e, sl=sl):
                for dj in range(DJ):
                    last = e.matmul(ps_sum[:, 0:TC], ones_bf[:], xcb[:, sl, dj],
                                    start=(dj == 0), stop=(dj == DJ - 1))
                return last
            sq.step("tensor", mm_sums, reads=[("xcb", sl), ("ones_bf",)],
                    writes=[("ps_sum",)])

            def mm_sumsq(e):
                for dj in range(DJ):
                    last = e.matmul(ps_sumsq[:, 0:TC], ones_bf[:], xsqb[:, dj],
                                    start=(dj == 0), stop=(dj == DJ - 1))
                return last
            sq.step("tensor", mm_sumsq, reads=[("xsqb",), ("ones_bf",)],
                    writes=[("ps_sumsq",)])

            sq.step("vector", lambda e: e.tensor_scalar(mrow[:], ps_sum[:, 0:TC], 1.0 / D, None, ALU.mult),
                    reads=[("ps_sum",)], writes=[("mrow",)])
            sq.step("vector", lambda e: e.tensor_tensor(rrow[:], mrow[:], mrow[:], ALU.mult),
                    reads=[("mrow",)], writes=[("rrow",)])
            sq.step("vector", lambda e: e.scalar_tensor_tensor(rrow[:], ps_sumsq[:, 0:TC], 1.0 / D, rrow[:], ALU.mult, ALU.subtract),
                    reads=[("ps_sumsq",), ("rrow",)], writes=[("rrow",)])
            sq.step("vector", lambda e: e.tensor_scalar(rrow[:], rrow[:], float(EPS), None, ALU.add),
                    reads=[("rrow",)], writes=[("rrow",)])
            sq.step("scalar", lambda e: e.sqrt(rrow[:], rrow[:]),
                    reads=[("rrow",)], writes=[("rrow",)])
            sq.step("vector", lambda e: e.reciprocal(rrow[:], rrow[:]),
                    reads=[("rrow",)], writes=[("rrow",)])
            sq.step("tensor", lambda e: e.matmul(ps_mbc[:, 0:TC], ones_f32r[:], mrow[:], start=True, stop=True),
                    reads=[("mrow",), ("ones_f32r",)], writes=[("ps_mbc",)])
            sq.step("tensor", lambda e: e.matmul(ps_rbc[:, 0:TC], ones_f32r[:], rrow[:], start=True, stop=True),
                    reads=[("rrow",), ("ones_f32r",)], writes=[("ps_rbc",)])
            def norm1(e):
                for dj in range(DJ):
                    last = e.tensor_tensor(xc[:, dj], xc[:, dj], ps_mbc[:, 0:TC], ALU.subtract)
                return last
            sq.step("vector", norm1, reads=[("xc",), ("ps_mbc",)], writes=[("xc",)])

            def norm2(e, sl=sl):
                for dj in range(DJ):
                    last = e.tensor_tensor(xnb[:, sl, dj], xc[:, dj], ps_rbc[:, 0:TC], ALU.mult)
                return last
            sq.step("vector", norm2, reads=[("xc",), ("ps_rbc",)], writes=[("xnb", sl)])
            store(sq, "scalar", _dj_view(xnT, t0, TC), xnb[:, sl],
                  reads=[("xnb", sl)], writes=[("hbm:xnT",)], ch="st_xnT")

        for wap, outap, wtok in ((wqT, qT, "wq"), (wkT, kT, "wk")):
            load_cast(sq, "gpsimd", w_sb[:],
                      wap.rearrange("(j p) e -> p j e", p=128),
                      reads=[], writes=[("w_sb",)], ch="w_sb")
            for it in range(NT):
                t0 = it * TC
                sl = it % 2
                load_cast(sq, "sync", xcb[:, sl], _dj_view(xnT, t0, TC),
                          reads=[("hbm:xnT",)], writes=[("xcb", sl)], ch="xcb%d" % sl)
                for ec in range(DJ):
                    bank = ec % 4

                    def mm(e, sl=sl, ec=ec, bank=bank):
                        for dj in range(DJ):
                            last = e.matmul(ps_o[:, bank, 0:TC], w_sb[:, dj, ec * 128:(ec + 1) * 128],
                                            xcb[:, sl, dj], start=(dj == 0), stop=(dj == DJ - 1))
                        return last
                    sq.step("tensor", mm, reads=[("w_sb",), ("xcb", sl)],
                            writes=[("ps_o", bank)])
                    sq.step("vector", lambda e, bank=bank: e.tensor_copy(ob[:, bank, 0:TC], ps_o[:, bank, 0:TC]),
                            reads=[("ps_o", bank)], writes=[("ob", bank)])
                    store(sq, "scalar", outap[ec * 128:(ec + 1) * 128, t0:t0 + TC], ob[:, bank, 0:TC],
                          reads=[("ob", bank)], writes=[("hbm:" + wtok,)], ch="st_" + wtok)

        load_cast(sq, "gpsimd", w_sb[:],
                  wvT.rearrange("(j p) e -> p j e", p=128),
                  reads=[], writes=[("w_sb",)], ch="w_sb")
        NEC = D // 512 if D >= 512 else 1
        ECW = min(512, D)
        for it in range(NT):
            t0 = it * TC
            sl = it % 2
            load_cast(sq, "sync", xcb[:, sl], _dj_view(xnT, t0, TC),
                      reads=[("hbm:xnT",)], writes=[("xcb", sl)], ch="xcb%d" % sl)
            for tt in range(TC // 128):
                for ec in range(NEC):
                    bank = ec % 4

                    def mm(e, sl=sl, tt=tt, ec=ec, bank=bank):
                        for dj in range(DJ):
                            last = e.matmul(ps_o[:, bank, 0:ECW],
                                            xcb[:, sl, dj, tt * 128:(tt + 1) * 128],
                                            w_sb[:, dj, ec * ECW:(ec + 1) * ECW],
                                            start=(dj == 0), stop=(dj == DJ - 1))
                        return last
                    sq.step("tensor", mm, reads=[("w_sb",), ("xcb", sl)],
                            writes=[("ps_o", bank)])
                    sq.step("vector", lambda e, bank=bank: e.tensor_copy(ob[:, bank, 0:ECW], ps_o[:, bank, 0:ECW]),
                            reads=[("ps_o", bank)], writes=[("ob", bank)])
                    store(sq, "scalar", V[t0 + tt * 128:t0 + tt * 128 + 128, ec * ECW:(ec + 1) * ECW],
                          ob[:, bank, 0:ECW], reads=[("ob", bank)], writes=[("hbm:V",)], ch="st_V")

        sq.add_final_wait()
        with nc.Block() as block:
            sq.emit(block)
    return nc



def _band_deltas(MP, QC):
    lo, hi = -(MP + 126), MP + QC - 2
    d0 = -((-lo - 1) // 128) * 128
    return [d for d in range(d0, hi, 128) if lo < d < hi]


def build_launch2(cfg):
    D, S = cfg["D"], cfg["S"]
    HH = cfg["L0_HEADS"] // 2
    HD = cfg["L0_HD"]
    MP = cfg["MAX_POS"]
    TABLE = 2 * MP - 1
    QC = min(512, S)
    NQC = S // QC
    KT = S // 128
    ISC = 1.0 / math.sqrt(HD)
    deltas = _band_deltas(MP, QC)
    NB = len(deltas)
    dmap = {d: i for i, d in enumerate(deltas)}
    GRP = 3

    nc = bass.Bass("TRN2", target_bir_lowering=False)
    qTh = nc.dram_tensor("qTh", [HH * HD, S], BF16, kind="ExternalInput").ap()
    kTh = nc.dram_tensor("kTh", [HH * HD, S], BF16, kind="ExternalInput").ap()
    Vh = nc.dram_tensor("Vh", [S, HH * HD], BF16, kind="ExternalInput").ap()
    relT = nc.dram_tensor("relT", [HD, TABLE], F32, kind="ExternalInput").ap()
    ctxTh = nc.dram_tensor("ctxTh", [HH * HD, S], BF16, kind="ExternalOutput").ap()
    PAD = 512
    mrev = nc.dram_tensor("mrev", [2 * S - 1 + 2 * PAD], F32, kind="Internal").ap()

    sq = Seq(nc)
    with contextlib.ExitStack() as es:
        sb = lambda name, shape, dt: es.enter_context(nc.sbuf_tensor(name, shape, dt))
        ps = lambda name, shape, dt=F32: es.enter_context(nc.psum_tensor(name, shape, dt))
        ones_bf = sb("ones_bf", [128, 1], BF16)
        ones_f32r = sb("ones_f32r", [1, 128], F32)
        ones_f32c = sb("ones_f32c", [128, 1], F32)
        relb = sb("relb", [HD, TABLE], F32)
        m_row = sb("m_row", [1, TABLE], F32)
        ms_row = sb("ms_row", [1, TABLE], F32)
        crow = sb("crow", [1, 512], F32)
        m0c = sb("m0c", [128, 1], F32)
        mLc = sb("mLc", [128, 1], F32)
        bias_sb = sb("bias_sb", [128, max(NB, 1), QC], F32)
        kths = sb("kths", [HD, 2, S], BF16)
        qths = sb("qths", [HD, 2, S], BF16)
        vhs = sb("vhs", [128, 2, KT, HD], BF16)
        Et = sb("Et", [128, 2 * GRP, 512], BF16)
        rec_row = sb("rec_row", [1, 512], F32)
        bc_sb = sb("bc_sb", [128, 512], F32)
        ob = sb("ob", [128, 2, 512], BF16)
        ps_s = ps("ps_s", [128, 2, GRP, 512])
        ps_ctx = ps("ps_ctx", [128, 512])
        ps_misc = ps("ps_misc", [128, 512])

        sq.step("gpsimd", lambda e: e.memset(ones_bf[:], 1.0), writes=[("ones_bf",)])
        sq.step("gpsimd", lambda e: e.memset(ones_f32r[:], 1.0), writes=[("ones_f32r",)])
        sq.step("gpsimd", lambda e: e.memset(ones_f32c[:], 1.0), writes=[("ones_f32c",)])

        load_cast(sq, "sync", relb[:], relT[:], reads=[], writes=[("relb",)], ch="relb")
        nmc = (TABLE + 511) // 512
        for i in range(nmc):
            c0 = i * 512
            cw = min(512, TABLE - c0)

            def mmm(e, c0=c0, cw=cw):
                return e.matmul(ps_misc[0:1, 0:cw], ones_f32c[0:HD, :], relb[:, c0:c0 + cw],
                                start=True, stop=True)
            sq.step("tensor", mmm, reads=[("relb",), ("ones_f32c",)], writes=[("ps_misc",)])
            sq.step("vector", lambda e, c0=c0, cw=cw: e.tensor_scalar(
                m_row[:, c0:c0 + cw], ps_misc[0:1, 0:cw], 1.0 / HD, None, ALU.mult),
                reads=[("ps_misc",)], writes=[("m_row",)])
            sq.step("vector", lambda e, c0=c0, cw=cw: e.tensor_scalar(
                ms_row[:, c0:c0 + cw], ps_misc[0:1, 0:cw], math.sqrt(HD) / HD, None, ALU.mult),
                reads=[("ps_misc",)], writes=[("ms_row",)])
        sq.step("tensor", lambda e: e.matmul(ps_misc[:, 0:1], ones_f32r[:], m_row[0:1, 0:1],
                                             start=True, stop=True),
                reads=[("m_row",), ("ones_f32r",)], writes=[("ps_misc",)])
        sq.step("vector", lambda e: e.tensor_copy(m0c[:], ps_misc[:, 0:1]),
                reads=[("ps_misc",)], writes=[("m0c",)])
        sq.step("tensor", lambda e: e.matmul(ps_misc[:, 0:1], ones_f32r[:],
                                             m_row[0:1, TABLE - 1:TABLE], start=True, stop=True),
                reads=[("m_row",), ("ones_f32r",)], writes=[("ps_misc",)])
        sq.step("vector", lambda e: e.tensor_copy(mLc[:], ps_misc[:, 0:1]),
                reads=[("ps_misc",)], writes=[("mLc",)])
        sq.step("gpsimd", lambda e: e.memset(crow[:], 1.0), writes=[("crow",)])
        sq.step("vector", lambda e: e.tensor_scalar(
            crow[:], crow[:], ms_row[0:1, 0:1], None, ALU.mult),
            reads=[("ms_row",), ("crow",)], writes=[("crow",)])
        nfill = PAD + S - MP
        off = 0
        while off < nfill:
            w = min(512, nfill - off)
            store(sq, "gpsimd", mrev[off:off + w], crow[0:1, 0:w],
                  reads=[("crow",)], writes=[("hbm:mrev",)], ch="st_mrev")
            off += w
        store(sq, "gpsimd", mrev[PAD + S - MP:PAD + S + MP - 1], ms_row[0:1, :],
              reads=[("ms_row",)], writes=[("hbm:mrev",)], ch="st_mrev")
        sq.step("gpsimd", lambda e: e.memset(crow[:], 1.0), reads=[("crow",)], writes=[("crow",)])
        sq.step("vector", lambda e: e.tensor_scalar(
            crow[:], crow[:], ms_row[0:1, TABLE - 1:TABLE], None, ALU.mult),
            reads=[("ms_row",), ("crow",)], writes=[("crow",)])
        off = PAD + S + MP - 1
        while off < 2 * S - 1 + 2 * PAD:
            w = min(512, 2 * S - 1 + 2 * PAD - off)
            store(sq, "gpsimd", mrev[off:off + w], crow[0:1, 0:w],
                  reads=[("crow",)], writes=[("hbm:mrev",)], ch="st_mrev")
            off += w
        for i, d in enumerate(deltas):
            def ldb(e, mark, i=i, d=d):
                srcap = bass.AP(mrev.tensor, PAD + S + d - QC, [[1, 128], [1, QC]])
                mark(e.dma_start(bias_sb[:, i], srcap))
            sq.step("sync", ldb, reads=[("hbm:mrev",)], writes=[("bias_sb",)],
                    dma=True, n_dma=1, ch="biast")

        for h in range(HH):
            hs = h % 2
            load_cast(sq, "sync", kths[:, hs], kTh[h * HD:(h + 1) * HD, :],
                      reads=[], writes=[("kths", hs)], ch="kths%d" % hs)
            load_cast(sq, "sync", qths[:, hs], qTh[h * HD:(h + 1) * HD, :],
                      reads=[], writes=[("qths", hs)], ch="qths%d" % hs)
            load_cast(sq, "sync", vhs[:, hs],
                      Vh.rearrange("(kt p) e -> p kt e", p=128)[:, :, h * HD:(h + 1) * HD],
                      reads=[], writes=[("vhs", hs)], ch="vhs%d" % hs)
            for qc in range(NQC):
                q0 = qc * QC
                left, band, right = [], [], []
                for kt in range(KT):
                    dlt = kt * 128 - q0
                    if dlt in dmap:
                        band.append(kt)
                    elif dlt < 0:
                        left.append(kt)
                    else:
                        right.append(kt)
                groups = []
                for cls, lst, bias in (("L", left, m0c), ("B", band, None), ("R", right, mLc)):
                    for i in range(0, len(lst), GRP):
                        groups.append((cls, lst[i:i + GRP], bias))
                n_av = 0
                total = KT
                for gi, (cls, ts, biasc) in enumerate(groups):
                    gs = gi % 2
                    n = len(ts)

                    def mm_s(e, ts=ts, gs=gs, hs=hs, q0=q0):
                        for i, kt in enumerate(ts):
                            last = e.matmul(ps_s[:, gs, i, 0:QC],
                                            kths[:, hs, kt * 128:(kt + 1) * 128],
                                            qths[:, hs, q0:q0 + QC],
                                            start=True, stop=True)
                        return last
                    sq.step("tensor", mm_s, reads=[("kths", hs), ("qths", hs)],
                            writes=[("ps_s", gs)])
                    if cls == "B":
                        def badd(e, ts=ts, gs=gs, q0=q0):
                            for i, kt in enumerate(ts):
                                bi = dmap[kt * 128 - q0]
                                last = e.tensor_tensor(ps_s[:, gs, i, 0:QC], ps_s[:, gs, i, 0:QC],
                                                       bias_sb[:, bi, ::-1], ALU.add)
                            return last
                        sq.step("vector", badd, reads=[("ps_s", gs), ("bias_sb",)],
                                writes=[("ps_s", gs)])

                    def expg(e, n=n, gs=gs, biasc=biasc):
                        bias = 0.0 if biasc is None else biasc[:]
                        if QC == 512:
                            return e.activation(Et[:, gs * GRP:gs * GRP + n, :],
                                                ps_s[:, gs, 0:n, :], AF.Exp,
                                                bias=bias, scale=ISC)
                        return e.activation(Et[:, gs * GRP:gs * GRP + n, 0:QC],
                                            ps_s[:, gs, 0:n, 0:QC], AF.Exp,
                                            bias=bias, scale=ISC)
                    rd = [("ps_s", gs)] + ([] if biasc is None else [(("m0c",) if biasc is m0c else ("mLc",))])
                    sq.step("scalar", expg, reads=rd, writes=[("Et", gs)])

                    def av(e, ts=ts, gs=gs, hs=hs, n_av=n_av, total=total):
                        for i, kt in enumerate(ts):
                            last = e.matmul(ps_ctx[0:HD, 0:QC], vhs[:, hs, kt],
                                            Et[:, gs * GRP + i, 0:QC],
                                            start=(n_av + i == 0), stop=(n_av + i == total - 1))
                        return last
                    sq.step("tensor", av, reads=[("Et", gs), ("vhs", hs)], writes=[("ps_ctx",)])

                    def sums(e, ts=ts, gs=gs, n_av=n_av, total=total):
                        for i, kt in enumerate(ts):
                            last = e.matmul(ps_misc[0:1, 0:QC], ones_bf[:],
                                            Et[:, gs * GRP + i, 0:QC],
                                            start=(n_av + i == 0), stop=(n_av + i == total - 1))
                        return last
                    sq.step("tensor", sums, reads=[("Et", gs), ("ones_bf",)], writes=[("ps_misc",)])
                    n_av += n

                sq.step("vector", lambda e: e.reciprocal(rec_row[0:1, 0:QC], ps_misc[0:1, 0:QC]),
                        reads=[("ps_misc",)], writes=[("rec_row",)])
                sq.step("tensor", lambda e: e.matmul(ps_misc[:, 0:QC], ones_f32r[:],
                                                     rec_row[0:1, 0:QC], start=True, stop=True),
                        reads=[("rec_row",), ("ones_f32r",)], writes=[("ps_misc",)])
                sq.step("vector", lambda e: e.tensor_copy(bc_sb[:, 0:QC], ps_misc[:, 0:QC]),
                        reads=[("ps_misc",)], writes=[("bc_sb",)])
                obs = qc % 2
                sq.step("vector", lambda e, obs=obs: e.tensor_tensor(
                    ob[0:HD, obs, 0:QC], ps_ctx[0:HD, 0:QC], bc_sb[0:HD, 0:QC], ALU.mult),
                    reads=[("ps_ctx",), ("bc_sb",)], writes=[("ob", obs)])
                store(sq, "scalar", ctxTh[h * HD:(h + 1) * HD, q0:q0 + QC], ob[0:HD, obs, 0:QC],
                      reads=[("ob", obs)], writes=[("hbm:ctx",)], ch="st_ctx")

        sq.add_final_wait()
        with nc.Block() as block:
            sq.emit(block)
    return nc



def _emit_gamma(sq, nc, wap, shape, bufs, name):
    R, C = shape
    RJ = R // 128
    wfp, acc, tmpc = bufs["gfp"], bufs["gacc"], bufs["gtmp"]
    ones_f32c, ones_f32r = bufs["ones_f32c"], bufs["ones_f32r"]
    ps_row = bufs["ps_misc"]
    tcol, tncol = bufs[name + "_t"], bufs[name + "_tn"]
    sq.step("gpsimd", lambda e: e.memset(acc[:], 0.0), reads=[("gacc",)], writes=[("gacc",)])
    wv = wap.rearrange("(j p) h -> p j h", p=128)
    for j in range(RJ):
        sl = j % 2
        load_cast(sq, "sync", wfp[:, sl, 0:C], wv[:, j], reads=[], writes=[("gfp", sl)],
                  ch="gfp%d" % sl)
        sq.step("vector", lambda e, sl=sl: e.tensor_reduce(
            tmpc[:], wfp[:, sl, 0:C], AX.X, ALU.add, apply_absolute_value=True),
            reads=[("gfp", sl)], writes=[("gtmp",)])
        sq.step("vector", lambda e: e.tensor_tensor(acc[:], acc[:], tmpc[:], ALU.add),
                reads=[("gtmp",), ("gacc",)], writes=[("gacc",)])
    sq.step("tensor", lambda e: e.matmul(ps_row[0:1, 0:1], acc[:], ones_f32c[:],
                                         start=True, stop=True),
            reads=[("gacc",), ("ones_f32c",)], writes=[("ps_misc",)])
    sq.step("vector", lambda e: e.tensor_scalar(
        tmpc[0:1, 0:1], ps_row[0:1, 0:1], 0.5 / (R * C), 0.5e-8, ALU.mult, ALU.add),
        reads=[("ps_misc",)], writes=[("gtmp",)])
    sq.step("tensor", lambda e: e.matmul(ps_row[:, 0:1], ones_f32r[:], tmpc[0:1, 0:1],
                                         start=True, stop=True),
            reads=[("gtmp",), ("ones_f32r",)], writes=[("ps_misc",)])
    sq.step("vector", lambda e: e.tensor_copy(tcol[:], ps_row[:, 0:1]),
            reads=[("ps_misc",)], writes=[(name + "_t",)])
    sq.step("vector", lambda e: e.tensor_scalar(tncol[:], tcol[:], -1.0, None, ALU.mult),
            reads=[(name + "_t",)], writes=[(name + "_tn",)])


def _emit_quant(sq, wq_ap, wfp_ap, s1_ap, s2_ap, tcol, tncol, toks, name_t, name_tn):
    rfp, w_s1, w_s2, w_q = toks
    sq.step("scalar", lambda e: e.activation(s1_ap, wfp_ap, AF.Sign, bias=tncol[:]),
            reads=[rfp, (name_tn,)], writes=[w_s1])
    sq.step("scalar", lambda e: e.activation(s2_ap, wfp_ap, AF.Sign, bias=tcol[:]),
            reads=[rfp, (name_t,)], writes=[w_s2])
    sq.step("vector", lambda e: e.tensor_tensor(wq_ap, s1_ap, s2_ap, ALU.add),
            reads=[w_s1, w_s2], writes=[w_q])


def build_launch3(cfg):
    D, T = cfg["D"], cfg["T"]
    H = cfg["FFN_H"]
    TC = min(512, T)
    DJ, NT = D // 128, T // TC
    HJ = H // 128
    nc = bass.Bass("TRN2", target_bir_lowering=False)
    xT = nc.dram_tensor("xT", [D, T], F32, kind="ExternalInput").ap()
    xnT = nc.dram_tensor("xnT", [D, T], BF16, kind="ExternalInput").ap()
    ctxT = nc.dram_tensor("ctxT", [D, T], BF16, kind="ExternalInput").ap()
    woT = nc.dram_tensor("woT", [D, D], F32, kind="ExternalInput").ap()
    w1T = nc.dram_tensor("w1T", [D, H], F32, kind="ExternalInput").ap()
    w2T = nc.dram_tensor("w2T", [D, H], F32, kind="ExternalInput").ap()
    w3T = nc.dram_tensor("w3T", [H, D], F32, kind="ExternalInput").ap()
    x1T = nc.dram_tensor("x1T", [D, T], F32, kind="ExternalOutput").ap()
    hidT = nc.dram_tensor("hidT", [H, T], BF16, kind="Internal").ap()
    ffnT = nc.dram_tensor("ffnT", [D, T], BF16, kind="Internal").ap()

    sq = Seq(nc)
    with contextlib.ExitStack() as es:
        sb = lambda name, shape, dt: es.enter_context(nc.sbuf_tensor(name, shape, dt))
        ps = lambda name, shape, dt=F32: es.enter_context(nc.psum_tensor(name, shape, dt))
        ones_f32c = sb("ones_f32c", [128, 1], F32)
        ones_f32r = sb("ones_f32r", [1, 128], F32)
        bufs = {"ones_f32c": ones_f32c, "ones_f32r": ones_f32r}
        for nm in ("w1", "w2", "w3"):
            bufs[nm + "_t"] = sb(nm + "_t", [128, 1], F32)
            bufs[nm + "_tn"] = sb(nm + "_tn", [128, 1], F32)
        ps_misc = ps("ps_misc", [128, 512])
        bufs["ps_misc"] = ps_misc
        sq.step("gpsimd", lambda e: e.memset(ones_f32c[:], 1.0), writes=[("ones_f32c",)])
        sq.step("gpsimd", lambda e: e.memset(ones_f32r[:], 1.0), writes=[("ones_f32r",)])

        def _stageA():
          with contextlib.ExitStack() as st_es:
            bufs["gfp"] = st_es.enter_context(nc.sbuf_tensor("gfp", [128, 2, H], F32))
            bufs["gacc"] = st_es.enter_context(nc.sbuf_tensor("gacc", [128, 1], F32))
            bufs["gtmp"] = st_es.enter_context(nc.sbuf_tensor("gtmp", [128, 1], F32))
            _emit_gamma(sq, nc, w1T, (D, H), bufs, "w1")
            _emit_gamma(sq, nc, w2T, (D, H), bufs, "w2")
            _emit_gamma(sq, nc, w3T, (H, D), bufs, "w3")
            sq.fence()
        _stageA()

        HC = min(512, H)
        NHC = H // HC
        def _stageB():
          with contextlib.ExitStack() as st_es:
            sb2 = lambda name, shape, dt: st_es.enter_context(nc.sbuf_tensor(name, shape, dt))
            xnb = sb2("xnb", [128, DJ, T], BF16)
            wfp = sb2("wfp", [128, DJ, HC], F32)
            s1b = sb2("s1b", [128, DJ, HC], BF16)
            s2b = sb2("s2b", [128, DJ, HC], BF16)
            w1q = sb2("w1q", [128, DJ, HC], BF16)
            w2q = sb2("w2q", [128, DJ, HC], BF16)
            sgb = sb2("sgb", [128, 2, 512], BF16)
            hob = sb2("hob", [128, 2, 512], BF16)
            ps_a = st_es.enter_context(nc.psum_tensor("ps_a", [128, 2, 512], F32))
            ps_b = st_es.enter_context(nc.psum_tensor("ps_b", [128, 2, 512], F32))
            load_cast(sq, "gpsimd", xnb[:], xnT.rearrange("(j p) t -> p j t", p=128),
                      reads=[], writes=[("xnb",)], ch="xnb")
            for hc in range(NHC):
                for wtag, wap, wq in (("w1", w1T, w1q), ("w2", w2T, w2q)):
                    load_cast(sq, "sync", wfp[:],
                              wap.rearrange("(j p) h -> p j h", p=128)[:, :, hc * HC:(hc + 1) * HC],
                              reads=[], writes=[("wfp",)], ch="wfp")
                    _emit_quant(sq, wq[:], wfp[:], s1b[:], s2b[:],
                                bufs[wtag + "_t"], bufs[wtag + "_tn"],
                                [("wfp",), ("s1b",), ("s2b",), (wtag + "q",)], wtag + "_t", wtag + "_tn")
                for tc in range(NT):
                    t0 = tc * TC
                    for sub in range(HC // 128):
                        pslot = sub % 2

                        def mma(e, hc=hc, tc=tc, sub=sub, pslot=pslot, t0=t0):
                            for dj in range(DJ):
                                last = e.matmul(ps_a[:, pslot, 0:TC],
                                                w1q[:, dj, sub * 128:(sub + 1) * 128],
                                                xnb[:, dj, t0:t0 + TC],
                                                start=(dj == 0), stop=(dj == DJ - 1))
                            return last
                        sq.step("tensor", mma, reads=[("w1q",), ("xnb",)], writes=[("ps_a", pslot)])

                        def mmb(e, hc=hc, tc=tc, sub=sub, pslot=pslot, t0=t0):
                            for dj in range(DJ):
                                last = e.matmul(ps_b[:, pslot, 0:TC],
                                                w2q[:, dj, sub * 128:(sub + 1) * 128],
                                                xnb[:, dj, t0:t0 + TC],
                                                start=(dj == 0), stop=(dj == DJ - 1))
                            return last
                        sq.step("tensor", mmb, reads=[("w2q",), ("xnb",)], writes=[("ps_b", pslot)])
                        sq.step("scalar", lambda e, pslot=pslot: e.activation(
                            sgb[:, pslot, 0:TC], ps_a[:, pslot, 0:TC], AF.Sigmoid, scale=0.5),
                            reads=[("ps_a", pslot)], writes=[("sgb", pslot)])
                        sq.step("vector", lambda e, pslot=pslot: e.scalar_tensor_tensor(
                            hob[:, pslot, 0:TC], ps_b[:, pslot, 0:TC], 0.5, sgb[:, pslot, 0:TC],
                            ALU.mult, ALU.mult),
                            reads=[("ps_b", pslot), ("sgb", pslot)], writes=[("hob", pslot)])
                        store(sq, "scalar", hidT[hc * HC + sub * 128:hc * HC + (sub + 1) * 128, t0:t0 + TC],
                              hob[:, pslot, 0:TC], reads=[("hob", pslot)],
                              writes=[("hbm:hidT",)], ch="st_hid")
            sq.fence()
        _stageB()

        EC3 = min(512, D)
        NEC3 = D // EC3
        def _stageC():
          with contextlib.ExitStack() as st_es:
            sb2 = lambda name, shape, dt: st_es.enter_context(nc.sbuf_tensor(name, shape, dt))
            w3fp = sb2("w3fp", [128, HJ, 128], F32)
            s1b = sb2("s1c", [128, HJ, 128], BF16)
            s2b = sb2("s2c", [128, HJ, 128], BF16)
            w3q = sb2("w3q", [128, HJ, EC3], BF16)
            hid = sb2("hid", [128, HJ, TC], BF16)
            fob = sb2("fob", [128, 2, 512], BF16)
            ps_a = st_es.enter_context(nc.psum_tensor("ps_a3", [128, 2, 512], F32))
            w3v = w3T.rearrange("(j p) ee -> p j ee", p=128)
            for ec in range(NEC3):
                for s4 in range(EC3 // 128):
                    c0 = ec * EC3 + s4 * 128
                    load_cast(sq, "sync", w3fp[:], w3v[:, :, c0:c0 + 128],
                              reads=[], writes=[("wfp",)], ch="wfp")
                    _emit_quant(sq, w3q[:, :, s4 * 128:(s4 + 1) * 128], w3fp[:], s1b[:], s2b[:],
                                bufs["w3_t"], bufs["w3_tn"],
                                [("wfp",), ("s1b",), ("s2b",), ("w3q", s4)], "w3_t", "w3_tn")
                for tc in range(NT):
                    t0 = tc * TC
                    load_cast(sq, "sync", hid[:],
                              hidT.rearrange("(j p) t -> p j t", p=128)[:, :, t0:t0 + TC],
                              reads=[("hbm:hidT",)], writes=[("hid",)], ch="hid")
                    for sub in range(EC3 // 128):
                        pslot = sub % 2

                        def mma(e, ec=ec, sub=sub, pslot=pslot):
                            for it in range(HJ):
                                last = e.matmul(ps_a[:, pslot, 0:TC],
                                                w3q[:, it, sub * 128:(sub + 1) * 128],
                                                hid[:, it],
                                                start=(it == 0), stop=(it == HJ - 1))
                            return last
                        sq.step("tensor", mma, reads=[("w3q", sub), ("hid",)], writes=[("ps_a", pslot)])
                        sq.step("vector", lambda e, pslot=pslot: e.tensor_scalar(
                            fob[:, pslot, 0:TC], ps_a[:, pslot, 0:TC], 0.5, None, ALU.mult),
                            reads=[("ps_a", pslot)], writes=[("fob", pslot)])
                        store(sq, "scalar", ffnT[ec * EC3 + sub * 128:ec * EC3 + (sub + 1) * 128, t0:t0 + TC],
                              fob[:, pslot, 0:TC], reads=[("fob", pslot)],
                              writes=[("hbm:ffnT",)], ch="st_ffn")
            sq.fence()
        _stageC()

        def _stageD():
          with contextlib.ExitStack() as st_es:
            sb2 = lambda name, shape, dt: st_es.enter_context(nc.sbuf_tensor(name, shape, dt))
            wo_sb = sb2("wo_sb", [128, DJ, D], BF16)
            cxb = sb2("cxb", [128, 2, DJ, TC], BF16)
            xfb = sb2("xfb", [128, 2, TC], F32)
            ffb = sb2("ffb", [128, 2, TC], BF16)
            t1b = sb2("t1b", [128, 2, TC], F32)
            ps_a = st_es.enter_context(nc.psum_tensor("ps_a4", [128, 2, 512], F32))
            load_cast(sq, "gpsimd", wo_sb[:], woT.rearrange("(j p) e -> p j e", p=128),
                      reads=[], writes=[("wo_sb",)], ch="wo_sb")
            for tc in range(NT):
                t0 = tc * TC
                sl = tc % 2
                load_cast(sq, "sync", cxb[:, sl], _dj_view(ctxT, t0, TC),
                          reads=[], writes=[("cxb", sl)], ch="cxb%d" % sl)
                for ec in range(DJ):
                    pslot = ec % 2
                    xsl = ec % 2

                    def mm(e, sl=sl, ec=ec, pslot=pslot):
                        for dj in range(DJ):
                            last = e.matmul(ps_a[:, pslot, 0:TC],
                                            wo_sb[:, dj, ec * 128:(ec + 1) * 128],
                                            cxb[:, sl, dj], start=(dj == 0), stop=(dj == DJ - 1))
                        return last
                    sq.step("tensor", mm, reads=[("wo_sb",), ("cxb", sl)], writes=[("ps_a", pslot)])
                    load_cast(sq, "sync", xfb[:, xsl], xT[ec * 128:(ec + 1) * 128, t0:t0 + TC],
                              reads=[], writes=[("xfb", xsl)], ch="xfb%d" % xsl)
                    load_cast(sq, "sync", ffb[:, xsl], ffnT[ec * 128:(ec + 1) * 128, t0:t0 + TC],
                              reads=[("hbm:ffnT",)], writes=[("ffb", xsl)], ch="ffb%d" % xsl)
                    sq.step("vector", lambda e, pslot=pslot, xsl=xsl: e.tensor_tensor(
                        t1b[:, xsl, 0:TC], ps_a[:, pslot, 0:TC], xfb[:, xsl, 0:TC], ALU.add),
                        reads=[("ps_a", pslot), ("xfb", xsl)], writes=[("t1b", xsl)])
                    sq.step("vector", lambda e, xsl=xsl: e.tensor_tensor(
                        t1b[:, xsl, 0:TC], t1b[:, xsl, 0:TC], ffb[:, xsl, 0:TC], ALU.add),
                        reads=[("t1b", xsl), ("ffb", xsl)], writes=[("t1b", xsl)])
                    store(sq, "scalar", x1T[ec * 128:(ec + 1) * 128, t0:t0 + TC],
                          t1b[:, xsl, 0:TC], reads=[("t1b", xsl)],
                          writes=[("hbm:x1T",)], ch="st_x1")
        _stageD()

        sq.add_final_wait()
        with nc.Block() as block:
            sq.emit(block)
    return nc



def _emit_ln(sq, nc, es, x_ap, xn_ap, D, T, TC, eps, consts):
    DJ, NT = D // 128, T // TC
    ones_bf, ones_f32r = consts["ones_bf"], consts["ones_f32r"]
    sb = lambda name, shape, dt: es.enter_context(nc.sbuf_tensor(name, shape, dt))
    xc = sb("ln_xc", [128, DJ, TC], F32)
    xcb = sb("ln_xcb", [128, 2, DJ, TC], BF16)
    xsqb = sb("ln_xsqb", [128, DJ, TC], BF16)
    xnb = sb("ln_xnb", [128, 2, DJ, TC], BF16)
    mrow = sb("ln_mrow", [1, TC], F32)
    rrow = sb("ln_rrow", [1, TC], F32)
    ps_sum = es.enter_context(nc.psum_tensor("ln_ps_sum", [1, 512], F32))
    ps_sumsq = es.enter_context(nc.psum_tensor("ln_ps_sumsq", [1, 512], F32))
    ps_mbc = es.enter_context(nc.psum_tensor("ln_ps_mbc", [128, 512], F32))
    ps_rbc = es.enter_context(nc.psum_tensor("ln_ps_rbc", [128, 512], F32))
    for it in range(NT):
        t0 = it * TC
        sl = it % 2
        load_cast(sq, "sync", xc[:], _dj_view(x_ap, t0, TC),
                  reads=[], writes=[("ln_xc",)], ch="ln_xc")
        load_cast(sq, "gpsimd", xcb[:, sl], _dj_view(x_ap, t0, TC),
                  reads=[], writes=[("ln_xcb", sl)], ch="ln_xcb%d" % sl)
        sq.step("scalar", lambda e, sl=sl: e.activation(xsqb[:], xcb[:, sl], AF.Square),
                reads=[("ln_xcb", sl)], writes=[("ln_xsqb",)])

        def mm_sums(e, sl=sl):
            for dj in range(DJ):
                last = e.matmul(ps_sum[:, 0:TC], ones_bf[:], xcb[:, sl, dj],
                                start=(dj == 0), stop=(dj == DJ - 1))
            return last
        sq.step("tensor", mm_sums, reads=[("ln_xcb", sl), ("ones_bf",)], writes=[("ln_ps_sum",)])

        def mm_sumsq(e):
            for dj in range(DJ):
                last = e.matmul(ps_sumsq[:, 0:TC], ones_bf[:], xsqb[:, dj],
                                start=(dj == 0), stop=(dj == DJ - 1))
            return last
        sq.step("tensor", mm_sumsq, reads=[("ln_xsqb",), ("ones_bf",)], writes=[("ln_ps_sumsq",)])
        sq.step("vector", lambda e: e.tensor_scalar(mrow[:], ps_sum[0:1, 0:TC], 1.0 / D, None, ALU.mult),
                reads=[("ln_ps_sum",)], writes=[("ln_mrow",)])
        sq.step("vector", lambda e: e.tensor_tensor(rrow[:], mrow[:], mrow[:], ALU.mult),
                reads=[("ln_mrow",)], writes=[("ln_rrow",)])
        sq.step("vector", lambda e: e.scalar_tensor_tensor(rrow[:], ps_sumsq[0:1, 0:TC], 1.0 / D,
                                                           rrow[:], ALU.mult, ALU.subtract),
                reads=[("ln_ps_sumsq",), ("ln_rrow",)], writes=[("ln_rrow",)])
        sq.step("vector", lambda e: e.tensor_scalar(rrow[:], rrow[:], float(eps), None, ALU.add),
                reads=[("ln_rrow",)], writes=[("ln_rrow",)])
        sq.step("scalar", lambda e: e.sqrt(rrow[:], rrow[:]),
                reads=[("ln_rrow",)], writes=[("ln_rrow",)])
        sq.step("vector", lambda e: e.reciprocal(rrow[:], rrow[:]),
                reads=[("ln_rrow",)], writes=[("ln_rrow",)])
        sq.step("tensor", lambda e: e.matmul(ps_mbc[:, 0:TC], ones_f32r[:], mrow[:], start=True, stop=True),
                reads=[("ln_mrow",), ("ones_f32r",)], writes=[("ln_ps_mbc",)])
        sq.step("tensor", lambda e: e.matmul(ps_rbc[:, 0:TC], ones_f32r[:], rrow[:], start=True, stop=True),
                reads=[("ln_rrow",), ("ones_f32r",)], writes=[("ln_ps_rbc",)])

        def norm1(e):
            for dj in range(DJ):
                last = e.tensor_tensor(xc[:, dj], xc[:, dj], ps_mbc[:, 0:TC], ALU.subtract)
            return last
        sq.step("vector", norm1, reads=[("ln_xc",), ("ln_ps_mbc",)], writes=[("ln_xc",)])

        def norm2(e, sl=sl):
            for dj in range(DJ):
                last = e.tensor_tensor(xnb[:, sl, dj], xc[:, dj], ps_rbc[:, 0:TC], ALU.mult)
            return last
        sq.step("vector", norm2, reads=[("ln_xc",), ("ln_ps_rbc",)], writes=[("ln_xnb", sl)])
        store(sq, "scalar", _dj_view(xn_ap, t0, TC), xnb[:, sl],
              reads=[("ln_xnb", sl)], writes=[("hbm:xn",)], ch="st_xn")


def build_launch4(cfg):
    D, T = cfg["D"], cfg["T"]
    H1, KVH = cfg["L1_H"], cfg["L1_KVH"]
    HD = cfg["L1_HD"]
    KVD = KVH * HD
    TC = min(512, T)
    DJ, NT = D // 128, T // TC
    KVJ = KVD // 128
    EPS = cfg["RMS_EPS"]
    BASE = cfg["BASE"]
    nc = bass.Bass("TRN2", target_bir_lowering=False)
    x1T = nc.dram_tensor("x1T", [D, T], F32, kind="ExternalInput").ap()
    wqT = nc.dram_tensor("wqT", [D, D], F32, kind="ExternalInput").ap()
    wkT = nc.dram_tensor("wkT", [D, KVD], F32, kind="ExternalInput").ap()
    wvT = nc.dram_tensor("wvT", [D, KVD], F32, kind="ExternalInput").ap()
    bqc = nc.dram_tensor("bqc", [128, DJ], F32, kind="ExternalInput").ap()
    bkc = nc.dram_tensor("bkc", [128, KVJ], F32, kind="ExternalInput").ap()
    bvr = nc.dram_tensor("bvr", [1, KVD], F32, kind="ExternalInput").ap()
    trow = nc.dram_tensor("trow", [1, T], F32, kind="ExternalInput").ap()
    xn1T = nc.dram_tensor("xn1T", [D, T], BF16, kind="ExternalOutput").ap()
    qhT = nc.dram_tensor("qhT", [D, T], BF16, kind="ExternalOutput").ap()
    khT = nc.dram_tensor("khT", [KVD, T], BF16, kind="ExternalOutput").ap()
    V1 = nc.dram_tensor("V1", [T, KVD], BF16, kind="ExternalOutput").ap()

    sq = Seq(nc)
    with contextlib.ExitStack() as es:
        sb = lambda name, shape, dt: es.enter_context(nc.sbuf_tensor(name, shape, dt))
        ones_bf = sb("ones_bf", [128, 1], BF16)
        ones_f32r = sb("ones_f32r", [1, 128], F32)
        consts = {"ones_bf": ones_bf, "ones_f32r": ones_f32r}
        sq.step("gpsimd", lambda e: e.memset(ones_bf[:], 1.0), writes=[("ones_bf",)])
        sq.step("gpsimd", lambda e: e.memset(ones_f32r[:], 1.0), writes=[("ones_f32r",)])
        Ct = sb("Ct", [128, T], F32)
        St = sb("St", [128, T], F32)
        trows = sb("trows", [1, T], F32)
        invf = sb("invf", [128, 1], F32)
        frq = sb("frq", [128, T], F32)
        mm1 = sb("mm1", [128, 512], F32)
        yb = sb("yb", [128, 512], F32)
        zb = sb("zb", [128, 512], F32)
        bq_sb = sb("bq_sb", [128, DJ], F32)
        bk_sb = sb("bk_sb", [128, KVJ], F32)
        bvbc = sb("bvbc", [128, KVD], F32)

        with contextlib.ExitStack() as st_es:
            _emit_ln(sq, nc, st_es, x1T, xn1T, D, T, TC, cfg["LN_EPS"], consts)
            sq.fence()

        with contextlib.ExitStack() as st_es:
            ps_b = st_es.enter_context(nc.psum_tensor("ps_rope", [128, 512], F32))
            load_cast(sq, "sync", trows[:], trow[:], reads=[], writes=[("trows",)], ch="trows")
            load_cast(sq, "sync", bq_sb[:], bqc[:], reads=[], writes=[("bq_sb",)], ch="bq")
            load_cast(sq, "sync", bk_sb[:], bkc[:], reads=[], writes=[("bk_sb",)], ch="bk")
            sq.step("gpsimd", lambda e: e.iota(invf[:], [[0, 1]], base=0, channel_multiplier=1,
                                              allow_small_or_imprecise_dtypes=True),
                    writes=[("invf",)])
            sq.step("vector", lambda e: e.tensor_scalar(invf[64:128, :], invf[64:128, :],
                                                        -64.0, None, ALU.add),
                    reads=[("invf",)], writes=[("invf",)])
            sq.step("vector", lambda e: e.tensor_scalar(invf[:], invf[:],
                                                        -math.log(BASE) / (HD // 2), None, ALU.mult),
                    reads=[("invf",)], writes=[("invf",)])
            sq.step("scalar", lambda e: e.activation(invf[:], invf[:], AF.Exp),
                    reads=[("invf",)], writes=[("invf",)])
            TPI = 2.0 * math.pi
            for it in range(NT):
                t0 = it * TC
                sq.step("tensor", lambda e, t0=t0: e.matmul(ps_b[:, 0:TC], ones_f32r[:],
                                                            trows[0:1, t0:t0 + TC], start=True, stop=True),
                        reads=[("trows",), ("ones_f32r",)], writes=[("ps_rope",)])
                sq.step("vector", lambda e, t0=t0: e.tensor_scalar(frq[:, t0:t0 + TC], ps_b[:, 0:TC],
                                                                   invf[:], None, ALU.mult),
                        reads=[("ps_rope",), ("invf",)], writes=[("frq", it)])
                for tab, phase, ttok in ((St, 0.0, "St"), (Ct, math.pi / 2.0, "Ct")):
                    dst = tab[:, t0:t0 + TC]
                    sq.step("vector", lambda e, phase=phase, t0=t0: e.tensor_scalar(
                        yb[:, 0:TC], frq[:, t0:t0 + TC], phase, None, ALU.add),
                        reads=[("frq", it)], writes=[("yb",)])
                    sq.step("vector", lambda e: e.tensor_scalar(
                        zb[:, 0:TC], yb[:, 0:TC], 1.0 / TPI, None, ALU.mult),
                        reads=[("yb",)], writes=[("zb",)])
                    sq.step("vector", lambda e: e.tensor_scalar(
                        mm1[:, 0:TC], zb[:, 0:TC], 8388608.0, -8388608.0, ALU.add, ALU.add),
                        reads=[("zb",)], writes=[("mm1",)])
                    sq.step("vector", lambda e: e.tensor_tensor(
                        zb[:, 0:TC], mm1[:, 0:TC], zb[:, 0:TC], ALU.is_gt),
                        reads=[("zb",), ("mm1",)], writes=[("zb",)])
                    sq.step("vector", lambda e: e.scalar_tensor_tensor(
                        mm1[:, 0:TC], zb[:, 0:TC], -1.0, mm1[:, 0:TC], ALU.mult, ALU.add),
                        reads=[("zb",), ("mm1",)], writes=[("mm1",)])
                    sq.step("vector", lambda e, dst=dst: e.scalar_tensor_tensor(
                        dst, mm1[:, 0:TC], -TPI, yb[:, 0:TC], ALU.mult, ALU.add),
                        reads=[("mm1",), ("yb",)], writes=[(ttok + "r", it)])
                    sq.step("vector", lambda e, dst=dst: e.tensor_single_scalar(
                        mm1[:, 0:TC], dst, math.pi, ALU.is_gt),
                        reads=[(ttok + "r", it)], writes=[("mm1",)])
                    sq.step("vector", lambda e, dst=dst: e.scalar_tensor_tensor(
                        dst, mm1[:, 0:TC], -TPI, dst, ALU.mult, ALU.add),
                        reads=[("mm1",), (ttok + "r", it)], writes=[(ttok + "r", it)])
                    sq.step("scalar", lambda e, dst=dst: e.activation(dst, dst, AF.Sin),
                        reads=[(ttok + "r", it)], writes=[(ttok + "r", it)])
            def negs(e):
                return e.tensor_scalar(St[0:64, :], St[0:64, :], -1.0, None, ALU.mult)
            sq.step("vector", negs, reads=[("Str", 0), ("Str", NT - 1)] + [("Str", i) for i in range(NT)],
                    writes=[("St_done",)])
            sq.fence()

        def qk_stage(wap, w_cols, bias_sb, bias_tok, out_ap, nheads, ch_tag):
            with contextlib.ExitStack() as st_es:
                sb2 = lambda name, shape, dt: st_es.enter_context(nc.sbuf_tensor(name, shape, dt))
                w_sb = sb2(ch_tag + "w_sb", [128, DJ, w_cols], BF16)
                xnb = sb2(ch_tag + "xnb", [128, 2, DJ, TC], BF16)
                kf = sb2(ch_tag + "kf", [128, TC], F32)
                sqb = sb2(ch_tag + "sqb", [128, TC], BF16)
                rr = sb2(ch_tag + "rr", [1, TC], F32)
                t1 = sb2(ch_tag + "t1", [128, TC], F32)
                t2 = sb2(ch_tag + "t2", [128, TC], F32)
                obuf = sb2(ch_tag + "ob", [128, 2, TC], BF16)
                ps_p = st_es.enter_context(nc.psum_tensor(ch_tag + "ps_p", [128, 2, 512], F32))
                ps_r = st_es.enter_context(nc.psum_tensor(ch_tag + "ps_r", [1, 512], F32))
                ps_rb = st_es.enter_context(nc.psum_tensor(ch_tag + "ps_rb", [128, 512], F32))
                load_cast(sq, "gpsimd", w_sb[:], wap.rearrange("(j p) e -> p j e", p=128),
                          reads=[], writes=[(ch_tag + "w",)], ch=ch_tag + "w")
                for tc in range(NT):
                    t0 = tc * TC
                    sl = tc % 2
                    load_cast(sq, "sync", xnb[:, sl], _dj_view(xn1T, t0, TC),
                              reads=[("hbm:xn",)], writes=[(ch_tag + "xnb", sl)], ch=ch_tag + "x%d" % sl)
                    for hh in range(nheads):
                        pslot = hh % 2

                        def mm(e, sl=sl, hh=hh, pslot=pslot):
                            for dj in range(DJ):
                                last = e.matmul(ps_p[:, pslot, 0:TC],
                                                w_sb[:, dj, hh * 128:(hh + 1) * 128],
                                                xnb[:, sl, dj], start=(dj == 0), stop=(dj == DJ - 1))
                            return last
                        sq.step("tensor", mm, reads=[(ch_tag + "w",), (ch_tag + "xnb", sl)],
                                writes=[(ch_tag + "pp", pslot)])
                        sq.step("vector", lambda e, hh=hh, pslot=pslot: e.tensor_scalar(
                            kf[:], ps_p[:, pslot, 0:TC], bias_sb[:, hh:hh + 1], None, ALU.add),
                            reads=[(ch_tag + "pp", pslot), bias_tok], writes=[(ch_tag + "kf",)])
                        sq.step("scalar", lambda e: e.activation(sqb[:], kf[:], AF.Square),
                                reads=[(ch_tag + "kf",)], writes=[(ch_tag + "sq",)])
                        sq.step("tensor", lambda e: e.matmul(ps_r[0:1, 0:TC], ones_bf[:], sqb[:],
                                                             start=True, stop=True),
                                reads=[(ch_tag + "sq",), ("ones_bf",)], writes=[(ch_tag + "pr",)])
                        sq.step("vector", lambda e: e.tensor_scalar(rr[:], ps_r[0:1, 0:TC],
                                                                    1.0 / HD, float(EPS), ALU.mult, ALU.add),
                                reads=[(ch_tag + "pr",)], writes=[(ch_tag + "rr",)])
                        sq.step("scalar", lambda e: e.sqrt(rr[:], rr[:]),
                                reads=[(ch_tag + "rr",)], writes=[(ch_tag + "rr",)])
                        sq.step("vector", lambda e: e.reciprocal(rr[:], rr[:]),
                                reads=[(ch_tag + "rr",)], writes=[(ch_tag + "rr",)])
                        sq.step("tensor", lambda e: e.matmul(ps_rb[:, 0:TC], ones_f32r[:], rr[:],
                                                             start=True, stop=True),
                                reads=[(ch_tag + "rr",), ("ones_f32r",)], writes=[(ch_tag + "prb",)])
                        sq.step("vector", lambda e, t0=t0: e.tensor_tensor(
                            t1[:], kf[:], Ct[:, t0:t0 + TC], ALU.mult),
                            reads=[(ch_tag + "kf",), ("Ctr", tc)], writes=[(ch_tag + "t1",)])

                        def swap(e, t0=t0):
                            e.tensor_tensor(t2[0:64, :], kf[64:128, :], St[64:128, t0:t0 + TC], ALU.mult)
                            return e.tensor_tensor(t2[64:128, :], kf[0:64, :], St[0:64, t0:t0 + TC], ALU.mult)
                        sq.step("vector", swap, reads=[(ch_tag + "kf",), ("St_done",)],
                                writes=[(ch_tag + "t2",)])
                        sq.step("vector", lambda e: e.tensor_tensor(t1[:], t1[:], t2[:], ALU.add),
                                reads=[(ch_tag + "t1",), (ch_tag + "t2",)], writes=[(ch_tag + "t1",)])
                        obs = hh % 2
                        sq.step("vector", lambda e, obs=obs: e.tensor_tensor(
                            obuf[:, obs], t1[:], ps_rb[:, 0:TC], ALU.mult),
                            reads=[(ch_tag + "t1",), (ch_tag + "prb",)], writes=[(ch_tag + "o", obs)])
                        store(sq, "scalar", out_ap[hh * 128:(hh + 1) * 128, t0:t0 + TC],
                              obuf[:, obs], reads=[(ch_tag + "o", obs)],
                              writes=[("hbm:" + ch_tag,)], ch="st_" + ch_tag)
                sq.fence()

        qk_stage(wkT, KVD, bk_sb, ("bk_sb",), khT, KVJ, "kk")
        qk_stage(wqT, D, bq_sb, ("bq_sb",), qhT, DJ, "qq")

        with contextlib.ExitStack() as st_es:
            sb2 = lambda name, shape, dt: st_es.enter_context(nc.sbuf_tensor(name, shape, dt))
            wv_sb = sb2("wv_sb", [128, DJ, KVD], BF16)
            xnb = sb2("v_xnb", [128, 2, DJ, TC], BF16)
            vob = sb2("v_ob", [128, 2, KVD], BF16)
            ps_v = st_es.enter_context(nc.psum_tensor("ps_v", [128, 2, 512], F32))
            load_cast(sq, "gpsimd", wv_sb[:], wvT.rearrange("(j p) e -> p j e", p=128),
                      reads=[], writes=[("wv",)], ch="wv")
            load_cast(sq, "sync", bvbc[0:1, :], bvr[:], reads=[], writes=[("bvr",)], ch="bvr")
            sq.step("tensor", lambda e: e.matmul(ps_v[:, 0, 0:KVD], ones_f32r[:], bvbc[0:1, :],
                                                 start=True, stop=True),
                    reads=[("bvr",), ("ones_f32r",)], writes=[("ps_v", 0)])
            sq.step("vector", lambda e: e.tensor_copy(bvbc[:], ps_v[:, 0, 0:KVD]),
                    reads=[("ps_v", 0)], writes=[("bvbc",)])
            for tc in range(NT):
                t0 = tc * TC
                sl = tc % 2
                load_cast(sq, "sync", xnb[:, sl], _dj_view(xn1T, t0, TC),
                          reads=[("hbm:xn",)], writes=[("v_xnb", sl)], ch="v_x%d" % sl)
                for tt in range(TC // 128):
                    pslot = tt % 2

                    def mmv(e, sl=sl, tt=tt, pslot=pslot):
                        for dj in range(DJ):
                            last = e.matmul(ps_v[:, pslot, 0:KVD],
                                            xnb[:, sl, dj, tt * 128:(tt + 1) * 128],
                                            wv_sb[:, dj, :], start=(dj == 0), stop=(dj == DJ - 1))
                        return last
                    sq.step("tensor", mmv, reads=[("wv",), ("v_xnb", sl), ("bvbc",)],
                            writes=[("ps_v", pslot)])
                    sq.step("vector", lambda e, pslot=pslot: e.tensor_tensor(
                        vob[:, pslot], ps_v[:, pslot, 0:KVD], bvbc[:], ALU.add),
                        reads=[("ps_v", pslot), ("bvbc",)], writes=[("v_ob", pslot)])
                    store(sq, "scalar", V1[t0 + tt * 128:t0 + tt * 128 + 128, :], vob[:, pslot],
                          reads=[("v_ob", pslot)], writes=[("hbm:V1",)], ch="st_V1")

        sq.add_final_wait()
        with nc.Block() as block:
            sq.emit(block)
    return nc


def build_launch5(cfg):
    D, S = cfg["D"], cfg["S"]
    H1, KVH = cfg["L1_H"], cfg["L1_KVH"]
    HD = cfg["L1_HD"]
    HHq = H1 // 2
    KVJh = KVH // 2
    REP = H1 // KVH
    QC = min(512, S)
    NQC = S // QC
    KPQ = QC // 128
    ISC = 1.0 / math.sqrt(HD)
    GRP = 3
    nc = bass.Bass("TRN2", target_bir_lowering=False)
    qhTh = nc.dram_tensor("qhTh", [HHq * HD, S], BF16, kind="ExternalInput").ap()
    khTh = nc.dram_tensor("khTh", [KVJh * 128, S], BF16, kind="ExternalInput").ap()
    Vh = nc.dram_tensor("Vh", [S, KVJh * 128], BF16, kind="ExternalInput").ap()
    qgr = nc.dram_tensor("qgr", [1, HHq], F32, kind="ExternalInput").ap()
    ctxTh = nc.dram_tensor("ctxTh", [HHq * HD, S], BF16, kind="ExternalOutput").ap()

    sq = Seq(nc)
    with contextlib.ExitStack() as es:
        sb = lambda name, shape, dt: es.enter_context(nc.sbuf_tensor(name, shape, dt))
        ones_bf = sb("ones_bf", [128, 1], BF16)
        ones_f32r = sb("ones_f32r", [1, 128], F32)
        masks = sb("masks", [128, KPQ, QC], BF16)
        scl = sb("scl", [128, HHq], F32)
        qgs = sb("qgs", [1, HHq], F32)
        kths = sb("kths", [128, 2, S], BF16)
        qths = sb("qths", [128, 2, S], BF16)
        vhs = sb("vhs", [128, 2, S // 128, HD], BF16)
        Et = sb("Et", [128, 2 * GRP, 512], BF16)
        rec_row = sb("rec_row", [1, 512], F32)
        bc_sb = sb("bc_sb", [128, 512], F32)
        ob = sb("ob", [128, 2, 512], BF16)
        ps_s = es.enter_context(nc.psum_tensor("ps_s", [128, 2, GRP, 512], F32))
        ps_ctx = es.enter_context(nc.psum_tensor("ps_ctx", [128, 512], F32))
        ps_misc = es.enter_context(nc.psum_tensor("ps_misc", [128, 512], F32))

        sq.step("gpsimd", lambda e: e.memset(ones_bf[:], 1.0), writes=[("ones_bf",)])
        sq.step("gpsimd", lambda e: e.memset(ones_f32r[:], 1.0), writes=[("ones_f32r",)])
        sq.step("gpsimd", lambda e: e.memset(masks[:], 1.0), writes=[("masks",)])
        for j in range(KPQ):
            sq.step("gpsimd", lambda e, j=j: e.affine_select(
                masks[:, j], masks[:, j], [[1, QC]], ALU.is_ge, 0.0,
                base=-j * 128, channel_multiplier=-1),
                reads=[("masks",)], writes=[("masks",)])
        load_cast(sq, "sync", qgs[:], qgr[:], reads=[], writes=[("qgs",)], ch="qg")
        sq.step("tensor", lambda e: e.matmul(ps_misc[:, 0:HHq], ones_f32r[:], qgs[:],
                                             start=True, stop=True),
                reads=[("qgs",), ("ones_f32r",)], writes=[("ps_misc",)])
        sq.step("vector", lambda e: e.tensor_scalar(scl[:], ps_misc[:, 0:HHq], ISC, None, ALU.mult),
                reads=[("ps_misc",)], writes=[("scl",)])

        for h in range(HHq):
            hs = h % 2
            kv = h // REP
            load_cast(sq, "sync", kths[:, hs], khTh[kv * 128:(kv + 1) * 128, :],
                      reads=[], writes=[("kths", hs)], ch="kths%d" % hs)
            load_cast(sq, "sync", qths[:, hs], qhTh[h * HD:(h + 1) * HD, :],
                      reads=[], writes=[("qths", hs)], ch="qths%d" % hs)
            load_cast(sq, "sync", vhs[:, hs],
                      Vh.rearrange("(kt p) e -> p kt e", p=128)[:, :, kv * HD:(kv + 1) * HD],
                      reads=[], writes=[("vhs", hs)], ch="vhs%d" % hs)
            for qc in range(NQC):
                q0 = qc * QC
                n_kt = (q0 + QC) // 128
                kts = list(range(n_kt))
                groups = [kts[i:i + GRP] for i in range(0, len(kts), GRP)]
                n_av = 0
                for gi, ts in enumerate(groups):
                    gs = gi % 2
                    n = len(ts)

                    def mm_s(e, ts=ts, gs=gs, hs=hs, q0=q0):
                        for i, kt in enumerate(ts):
                            last = e.matmul(ps_s[:, gs, i, 0:QC],
                                            kths[:, hs, kt * 128:(kt + 1) * 128],
                                            qths[:, hs, q0:q0 + QC],
                                            start=True, stop=True)
                        return last
                    sq.step("tensor", mm_s, reads=[("kths", hs), ("qths", hs)],
                            writes=[("ps_s", gs)])

                    def expg(e, n=n, gs=gs, h=h):
                        if QC == 512:
                            return e.activation(Et[:, gs * GRP:gs * GRP + n, :],
                                                ps_s[:, gs, 0:n, :], AF.Exp,
                                                bias=0.0, scale=scl[:, h:h + 1])
                        return e.activation(Et[:, gs * GRP:gs * GRP + n, 0:QC],
                                            ps_s[:, gs, 0:n, 0:QC], AF.Exp,
                                            bias=0.0, scale=scl[:, h:h + 1])
                    sq.step("scalar", expg, reads=[("ps_s", gs), ("scl",)], writes=[("Et", gs)])
                    diag = [(i, kt) for i, kt in enumerate(ts) if kt >= n_kt - KPQ]
                    if diag:
                        def maskg(e, diag=diag, gs=gs, n_kt=n_kt):
                            for i, kt in diag:
                                j = kt - (n_kt - KPQ)
                                last = e.tensor_tensor(Et[:, gs * GRP + i, 0:QC],
                                                       Et[:, gs * GRP + i, 0:QC],
                                                       masks[:, j, 0:QC], ALU.mult)
                            return last
                        sq.step("vector", maskg, reads=[("Et", gs), ("masks",)], writes=[("Et", gs)])

                    def av(e, ts=ts, gs=gs, hs=hs, n_av=n_av, total=n_kt):
                        for i, kt in enumerate(ts):
                            last = e.matmul(ps_ctx[:, 0:QC], vhs[:, hs, kt],
                                            Et[:, gs * GRP + i, 0:QC],
                                            start=(n_av + i == 0), stop=(n_av + i == total - 1))
                        return last
                    sq.step("tensor", av, reads=[("Et", gs), ("vhs", hs)], writes=[("ps_ctx",)])

                    def sums(e, ts=ts, gs=gs, n_av=n_av, total=n_kt):
                        for i, kt in enumerate(ts):
                            last = e.matmul(ps_misc[0:1, 0:QC], ones_bf[:],
                                            Et[:, gs * GRP + i, 0:QC],
                                            start=(n_av + i == 0), stop=(n_av + i == total - 1))
                        return last
                    sq.step("tensor", sums, reads=[("Et", gs), ("ones_bf",)], writes=[("ps_misc",)])
                    n_av += n

                sq.step("vector", lambda e: e.reciprocal(rec_row[0:1, 0:QC], ps_misc[0:1, 0:QC]),
                        reads=[("ps_misc",)], writes=[("rec_row",)])
                sq.step("tensor", lambda e: e.matmul(ps_misc[:, 0:QC], ones_f32r[:],
                                                     rec_row[0:1, 0:QC], start=True, stop=True),
                        reads=[("rec_row",), ("ones_f32r",)], writes=[("ps_misc",)])
                sq.step("vector", lambda e: e.tensor_copy(bc_sb[:, 0:QC], ps_misc[:, 0:QC]),
                        reads=[("ps_misc",)], writes=[("bc_sb",)])
                obs = qc % 2
                sq.step("vector", lambda e, obs=obs: e.tensor_tensor(
                    ob[:, obs, 0:QC], ps_ctx[:, 0:QC], bc_sb[:, 0:QC], ALU.mult),
                    reads=[("ps_ctx",), ("bc_sb",)], writes=[("ob", obs)])
                store(sq, "scalar", ctxTh[h * HD:(h + 1) * HD, q0:q0 + QC], ob[:, obs, 0:QC],
                      reads=[("ob", obs)], writes=[("hbm:ctx",)], ch="st_ctx")

        sq.add_final_wait()
        with nc.Block() as block:
            sq.emit(block)
    return nc


def build_launch6(cfg):
    D, T = cfg["D"], cfg["T"]
    E = cfg["E"]
    I = cfg["INTER"]
    TC = min(512, T)
    DJ, NT = D // 128, T // TC
    IJ = I // 128
    nc = bass.Bass("TRN2", target_bir_lowering=False)
    x1T = nc.dram_tensor("x1T", [D, T], F32, kind="ExternalInput").ap()
    xn1T = nc.dram_tensor("xn1T", [D, T], BF16, kind="ExternalInput").ap()
    ctxT = nc.dram_tensor("ctxT", [D, T], BF16, kind="ExternalInput").ap()
    woT = nc.dram_tensor("woT", [D, D], F32, kind="ExternalInput").ap()
    boc = nc.dram_tensor("boc", [128, DJ], F32, kind="ExternalInput").ap()
    gwT = nc.dram_tensor("gwT", [D, E], F32, kind="ExternalInput").ap()
    w1T = nc.dram_tensor("w1T", [E, D, I], F32, kind="ExternalInput").ap()
    w2T = nc.dram_tensor("w2T", [E, I, D], F32, kind="ExternalInput").ap()
    w3T = nc.dram_tensor("w3T", [E, D, I], F32, kind="ExternalInput").ap()
    ws1T = nc.dram_tensor("ws1T", [D, I], F32, kind="ExternalInput").ap()
    ws2T = nc.dram_tensor("ws2T", [I, D], F32, kind="ExternalInput").ap()
    ws3T = nc.dram_tensor("ws3T", [D, I], F32, kind="ExternalInput").ap()
    outT = nc.dram_tensor("outT", [D, T], F32, kind="ExternalOutput").ap()
    attT = nc.dram_tensor("attT", [D, T], BF16, kind="Internal").ap()
    moeT = nc.dram_tensor("moeT", [D, T], F32, kind="Internal").ap()
    cTh = nc.dram_tensor("cTh", [E, T], F32, kind="Internal").ap()

    sq = Seq(nc)
    with contextlib.ExitStack() as es:
        sb = lambda name, shape, dt: es.enter_context(nc.sbuf_tensor(name, shape, dt))
        ones_bf = sb("ones_bf", [128, 1], BF16)
        ones_f32r = sb("ones_f32r", [1, 128], F32)
        ident = sb("ident", [128, 128], F32)
        cT_sb = sb("cT_sb", [E, T], F32)
        bo_sb = sb("bo_sb", [128, DJ], F32)
        sq.step("gpsimd", lambda e: e.memset(ones_bf[:], 1.0), writes=[("ones_bf",)])
        sq.step("gpsimd", lambda e: e.memset(ones_f32r[:], 1.0), writes=[("ones_f32r",)])
        sq.step("gpsimd", lambda e: e.memset(ident[:], 1.0), writes=[("ident",)])
        sq.step("gpsimd", lambda e: e.affine_select(ident[:], ident[:], [[-1, 128]],
                                                    ALU.is_equal, 0.0, base=0, channel_multiplier=1),
                reads=[("ident",)], writes=[("ident",)])
        load_cast(sq, "sync", bo_sb[:], boc[:], reads=[], writes=[("bo_sb",)], ch="bo")

        def _stageA():
          with contextlib.ExitStack() as st_es:
            sb2 = lambda name, shape, dt: st_es.enter_context(nc.sbuf_tensor(name, shape, dt))
            wo_sb = sb2("wo_sb", [128, DJ, D], BF16)
            cxb = sb2("cxb", [128, 2, DJ, TC], BF16)
            aob = sb2("aob", [128, 2, TC], BF16)
            ps_a = st_es.enter_context(nc.psum_tensor("ps_wo", [128, 2, 512], F32))
            load_cast(sq, "gpsimd", wo_sb[:], woT.rearrange("(j p) e -> p j e", p=128),
                      reads=[], writes=[("wo_sb",)], ch="wo_sb")
            for tc in range(NT):
                t0 = tc * TC
                sl = tc % 2
                load_cast(sq, "sync", cxb[:, sl], _dj_view(ctxT, t0, TC),
                          reads=[], writes=[("cxb", sl)], ch="cxb%d" % sl)
                for ec in range(DJ):
                    pslot = ec % 2

                    def mm(e, sl=sl, ec=ec, pslot=pslot):
                        for dj in range(DJ):
                            last = e.matmul(ps_a[:, pslot, 0:TC],
                                            wo_sb[:, dj, ec * 128:(ec + 1) * 128],
                                            cxb[:, sl, dj], start=(dj == 0), stop=(dj == DJ - 1))
                        return last
                    sq.step("tensor", mm, reads=[("wo_sb",), ("cxb", sl)], writes=[("ps_wo", pslot)])
                    obs = ec % 2
                    sq.step("vector", lambda e, ec=ec, pslot=pslot, obs=obs: e.tensor_scalar(
                        aob[:, obs], ps_a[:, pslot, 0:TC], bo_sb[:, ec:ec + 1], None, ALU.add),
                        reads=[("ps_wo", pslot), ("bo_sb",)], writes=[("aob", obs)])
                    store(sq, "scalar", attT[ec * 128:(ec + 1) * 128, t0:t0 + TC], aob[:, obs],
                          reads=[("aob", obs)], writes=[("hbm:att",)], ch="st_att")
            sq.fence()
        _stageA()

        def _stageB():
          with contextlib.ExitStack() as st_es:
            sb2 = lambda name, shape, dt: st_es.enter_context(nc.sbuf_tensor(name, shape, dt))
            gw_sb = sb2("gw_sb", [128, DJ, E], BF16)
            xnb = sb2("r_xnb", [128, 2, DJ, TC], BF16)
            pr = sb2("pr", [128, E], F32)
            c1 = sb2("c1", [128, 1], F32)
            c2 = sb2("c2", [128, 1], F32)
            g1 = sb2("g1", [128, E], F32)
            ps_g = st_es.enter_context(nc.psum_tensor("ps_g", [128, 512], F32))
            ps_t = st_es.enter_context(nc.psum_tensor("ps_t", [128, 512], F32))
            load_cast(sq, "gpsimd", gw_sb[:], gwT.rearrange("(j p) e -> p j e", p=128),
                      reads=[], writes=[("gw",)], ch="gw")
            for tc in range(NT):
                t0 = tc * TC
                sl = tc % 2
                load_cast(sq, "sync", xnb[:, sl], _dj_view(xn1T, t0, TC),
                          reads=[], writes=[("r_xnb", sl)], ch="r_x%d" % sl)
                for tt in range(TC // 128):
                    def mmg(e, sl=sl, tt=tt):
                        for dj in range(DJ):
                            last = e.matmul(ps_g[:, 0:E], xnb[:, sl, dj, tt * 128:(tt + 1) * 128],
                                            gw_sb[:, dj, :], start=(dj == 0), stop=(dj == DJ - 1))
                        return last
                    sq.step("tensor", mmg, reads=[("gw",), ("r_xnb", sl)], writes=[("ps_g",)])
                    sq.step("vector", lambda e: e.tensor_reduce(c1[:], ps_g[:, 0:E], AX.X,
                                                                ALU.max, negate=True),
                            reads=[("ps_g",)], writes=[("c1",)])
                    sq.step("scalar", lambda e: e.activation(pr[:], ps_g[:, 0:E], AF.Exp,
                                                             bias=c1[:], scale=1.0),
                            reads=[("ps_g",), ("c1",)], writes=[("pr",)])
                    sq.step("vector", lambda e: e.tensor_reduce(c2[:], pr[:], AX.X, ALU.add),
                            reads=[("pr",)], writes=[("c2",)])
                    sq.step("vector", lambda e: e.reciprocal(c2[:], c2[:]),
                            reads=[("c2",)], writes=[("c2",)])
                    sq.step("vector", lambda e: e.tensor_scalar(pr[:], pr[:], c2[:], None, ALU.mult),
                            reads=[("pr",), ("c2",)], writes=[("pr",)])
                    sq.step("vector", lambda e: e.tensor_reduce(c1[:], pr[:], AX.X, ALU.max),
                            reads=[("pr",)], writes=[("c1",)])
                    sq.step("vector", lambda e: e.tensor_scalar(g1[:], pr[:], c1[:], None, ALU.is_ge),
                            reads=[("pr",), ("c1",)], writes=[("g1",)])
                    sq.step("vector", lambda e: e.scalar_tensor_tensor(g1[:], g1[:], -3.0e38, pr[:],
                                                                       ALU.mult, ALU.add),
                            reads=[("g1",), ("pr",)], writes=[("g1",)])
                    sq.step("vector", lambda e: e.tensor_reduce(c2[:], g1[:], AX.X, ALU.max),
                            reads=[("g1",)], writes=[("c2",)])
                    sq.step("vector", lambda e: e.tensor_scalar(g1[:], pr[:], c2[:], None, ALU.is_ge),
                            reads=[("pr",), ("c2",)], writes=[("g1",)])
                    sq.step("vector", lambda e: e.tensor_tensor(pr[:], pr[:], g1[:], ALU.mult),
                            reads=[("pr",), ("g1",)], writes=[("pr",)])
                    sq.step("tensor", lambda e: e.transpose(ps_t[0:E, 0:128], pr[:], ident[:]),
                            reads=[("pr",), ("ident",)], writes=[("ps_t",)])
                    sq.step("vector", lambda e, t0=t0, tt=tt: e.tensor_copy(
                        cT_sb[:, t0 + tt * 128:t0 + tt * 128 + 128], ps_t[0:E, 0:128]),
                        reads=[("ps_t",)], writes=[("cT",)])
            store(sq, "scalar", cTh[:], cT_sb[:], reads=[("cT",)],
                  writes=[("hbm:cT",)], ch="st_cT")
            sq.fence()
        _stageB()

        def _expert(ex):
            shared = (ex == E)
            with contextlib.ExitStack() as st_es:
                sb2 = lambda name, shape, dt: st_es.enter_context(nc.sbuf_tensor("e%d" % ex + name, shape, dt))
                w1e = sb2("w1e", [128, DJ, I], BF16)
                w3e = sb2("w3e", [128, DJ, I], BF16)
                g_sb = sb2("g_sb", [128, IJ, T], BF16)
                xnb = sb2("xnb", [128, 2, DJ, TC], BF16)
                sil = sb2("sil", [128, TC], BF16)
                tg = sb2("tg", [128, TC], F32)
                cbc = sb2("cbc", [128, TC], F32)
                crow_e = sb2("crow_e", [1, 512], F32)
                mob = sb2("mob", [128, 2, TC], F32)
                ps_h = st_es.enter_context(nc.psum_tensor("e%d_ps_h" % ex, [128, 2, 512], F32))
                ps_u = st_es.enter_context(nc.psum_tensor("e%d_ps_u" % ex, [128, 2, 512], F32))
                ps_o = st_es.enter_context(nc.psum_tensor("e%d_ps_o" % ex, [128, 2, 512], F32))
                ps_cb = st_es.enter_context(nc.psum_tensor("e%d_ps_cb" % ex, [128, 512], F32))
                if shared:
                    w1ap, w3ap, w2ap = ws1T, ws3T, ws2T
                else:
                    w1ap, w3ap, w2ap = w1T[ex], w3T[ex], w2T[ex]
                load_cast(sq, "gpsimd", w1e[:], w1ap.rearrange("(j p) i -> p j i", p=128),
                          reads=[], writes=[("w1e",)], ch="we1")
                load_cast(sq, "gpsimd", w3e[:], w3ap.rearrange("(j p) i -> p j i", p=128),
                          reads=[], writes=[("w3e",)], ch="we3")
                for tc in range(NT):
                    t0 = tc * TC
                    sl = tc % 2
                    load_cast(sq, "sync", xnb[:, sl], _dj_view(xn1T, t0, TC),
                              reads=[], writes=[("exnb", sl)], ch="e_x%d" % sl)
                    if not shared:
                        load_cast(sq, "sync", crow_e[0:1, 0:TC], cTh[ex:ex + 1, t0:t0 + TC],
                                  reads=[("hbm:cT",)], writes=[("crow_e",)], ch="ldcrow")
                        sq.step("tensor", lambda e: e.matmul(
                            ps_cb[:, 0:TC], ones_f32r[:], crow_e[0:1, 0:TC],
                            start=True, stop=True),
                            reads=[("crow_e",), ("ones_f32r",)], writes=[("ps_cb",)])
                        sq.step("vector", lambda e: e.tensor_copy(cbc[:], ps_cb[:, 0:TC]),
                                reads=[("ps_cb",)], writes=[("cbc",)])
                    for it in range(IJ):
                        pslot = it % 2

                        def mmh(e, sl=sl, it=it, pslot=pslot):
                            for dj in range(DJ):
                                last = e.matmul(ps_h[:, pslot, 0:TC],
                                                w1e[:, dj, it * 128:(it + 1) * 128],
                                                xnb[:, sl, dj], start=(dj == 0), stop=(dj == DJ - 1))
                            return last
                        sq.step("tensor", mmh, reads=[("w1e",), ("exnb", sl)], writes=[("ps_h", pslot)])

                        def mmu(e, sl=sl, it=it, pslot=pslot):
                            for dj in range(DJ):
                                last = e.matmul(ps_u[:, pslot, 0:TC],
                                                w3e[:, dj, it * 128:(it + 1) * 128],
                                                xnb[:, sl, dj], start=(dj == 0), stop=(dj == DJ - 1))
                            return last
                        sq.step("tensor", mmu, reads=[("w3e",), ("exnb", sl)], writes=[("ps_u", pslot)])
                        sq.step("scalar", lambda e, pslot=pslot: e.activation(
                            sil[:], ps_h[:, pslot, 0:TC], AF.Silu),
                            reads=[("ps_h", pslot)], writes=[("sil",)])
                        if shared:
                            sq.step("vector", lambda e, it=it, t0=t0, pslot=pslot: e.tensor_tensor(
                                g_sb[:, it, t0:t0 + TC], ps_u[:, pslot, 0:TC], sil[:], ALU.mult),
                                reads=[("ps_u", pslot), ("sil",)], writes=[("g_sb", it, tc)])
                        else:
                            sq.step("vector", lambda e, pslot=pslot: e.tensor_tensor(
                                tg[:], ps_u[:, pslot, 0:TC], sil[:], ALU.mult),
                                reads=[("ps_u", pslot), ("sil",)], writes=[("tg",)])
                            sq.step("vector", lambda e, it=it, t0=t0: e.tensor_tensor(
                                g_sb[:, it, t0:t0 + TC], tg[:], cbc[:], ALU.mult),
                                reads=[("tg",), ("cbc",)], writes=[("g_sb", it, tc)])
                w2dst = w1e[:].rearrange("p a b -> p (a b)").rearrange(
                    "p (j d2) -> p j d2", j=IJ)
                load_cast(sq, "gpsimd", w2dst, w2ap.rearrange("(j p) d -> p j d", p=128),
                          reads=[("w1e",)], writes=[("w1e",)], ch="we1")
                for tc in range(NT):
                    t0 = tc * TC
                    for ec in range(DJ):
                        pslot = ec % 2

                        def mmo(e, ec=ec, tc=tc, t0=t0, pslot=pslot):
                            for it in range(IJ):
                                last = e.matmul(ps_o[:, pslot, 0:TC],
                                                _w2slice(w1e, IJ, it, ec),
                                                g_sb[:, it, t0:t0 + TC],
                                                start=(it == 0), stop=(it == IJ - 1))
                            return last
                        sq.step("tensor", mmo, reads=[("w1e",)] + [("g_sb", it, tc) for it in range(IJ)],
                                writes=[("ps_o", pslot)])
                        obs = ec % 2
                        sq.step("vector", lambda e, pslot=pslot, obs=obs: e.tensor_copy(
                            mob[:, obs], ps_o[:, pslot, 0:TC]),
                            reads=[("ps_o", pslot)], writes=[("mob", obs)])

                        def acc(e, mark, ec=ec, t0=t0, obs=obs, ex=ex):
                            op = ALU.bypass if ex == 0 else ALU.add
                            mark(e.dma_start(moeT[ec * 128:(ec + 1) * 128, t0:t0 + TC],
                                             mob[:, obs], accum_op=op))
                        sq.step("gpsimd", acc, reads=[("mob", obs)],
                                writes=[("hbm:moe", ec, tc)], dma=True, n_dma=1,
                                ch="moe%d" % ((ec * NT + tc) % 4))
                sq.fence()
        for ex_ in range(E + 1):
            _expert(ex_)

        def _stageD():
          with contextlib.ExitStack() as st_es:
            sb2 = lambda name, shape, dt: st_es.enter_context(nc.sbuf_tensor(name, shape, dt))
            xfb = sb2("f_x", [128, 2, TC], F32)
            afb = sb2("f_a", [128, 2, TC], BF16)
            mfb = sb2("f_m", [128, 2, TC], F32)
            t1b = sb2("f_t", [128, 2, TC], F32)
            for tc in range(NT):
                t0 = tc * TC
                for ec in range(DJ):
                    sl = ec % 2
                    load_cast(sq, "sync", xfb[:, sl], x1T[ec * 128:(ec + 1) * 128, t0:t0 + TC],
                              reads=[], writes=[("f_x", sl)], ch="f_x%d" % sl)
                    load_cast(sq, "sync", afb[:, sl], attT[ec * 128:(ec + 1) * 128, t0:t0 + TC],
                              reads=[("hbm:att",)], writes=[("f_a", sl)], ch="f_a%d" % sl)
                    load_cast(sq, "sync", mfb[:, sl], moeT[ec * 128:(ec + 1) * 128, t0:t0 + TC],
                              reads=[("hbm:moe", ec, tc)], writes=[("f_m", sl)], ch="f_m%d" % sl)
                    sq.step("vector", lambda e, sl=sl: e.tensor_tensor(
                        t1b[:, sl], xfb[:, sl], afb[:, sl], ALU.add),
                        reads=[("f_x", sl), ("f_a", sl)], writes=[("f_t", sl)])
                    sq.step("vector", lambda e, sl=sl: e.tensor_tensor(
                        t1b[:, sl], t1b[:, sl], mfb[:, sl], ALU.add),
                        reads=[("f_t", sl), ("f_m", sl)], writes=[("f_t", sl)])
                    store(sq, "scalar", outT[ec * 128:(ec + 1) * 128, t0:t0 + TC], t1b[:, sl],
                          reads=[("f_t", sl)], writes=[("hbm:out",)], ch="st_out")
        _stageD()

        sq.add_final_wait()
        with nc.Block() as block:
            sq.emit(block)
    return nc


def _w2slice(w1e, IJ, it, ec):
    flat = w1e[:].rearrange("p a b -> p (a b)")
    return flat.rearrange("p (j d) -> p j d", j=IJ)[:, it, ec * 128:(ec + 1) * 128]


def _cc(a):
    return np.ascontiguousarray(a)


def _device_run(nc, in_maps):
    res = run_bass_kernel_spmd(nc, in_maps, core_ids=list(range(len(in_maps))))
    return res.results


def kernel(**inputs):
    cfg = dict(CFG_FULL)
    cfg["T"] = cfg["S"] // 2
    return _kernel_impl(cfg, inputs, _device_run)


def _kernel_impl(cfg, inputs, run):
    B, S, D = cfg["B"], cfg["S"], cfg["D"]
    T = cfg["T"]
    HD0 = cfg["L0_HD"]
    HH0 = cfg["L0_HEADS"] // 2
    H1, KVH, HD1 = cfg["L1_H"], cfg["L1_KVH"], cfg["L1_HD"]
    HHq = H1 // 2
    KVD = KVH * HD1
    KVJh = KVH // 2
    DJ = D // 128
    KVJ = KVD // 128
    NC = 2 * B
    cores = [(c // 2, c % 2) for c in range(NC)]

    x = inputs["L0_attn_hidden_states"]
    w1m = {
        "wqT": _cc(inputs["L0_attn_wq"].T), "wkT": _cc(inputs["L0_attn_wk"].T),
        "wvT": _cc(inputs["L0_attn_wv"].T),
    }
    im = [dict(w1m, xT=_cc(x[b].T[:, hh * T:(hh + 1) * T])) for b, hh in cores]
    r1 = run(build_launch1(cfg), im)

    relT = _cc(inputs["L0_attn_rel_pos"].T)
    im = []
    for c, (b, hh) in enumerate(cores):
        rows = slice(hh * HH0 * HD0, (hh + 1) * HH0 * HD0)
        pa, pb = r1[2 * b], r1[2 * b + 1]
        im.append({
            "qTh": _cc(np.concatenate([pa["qT"][rows], pb["qT"][rows]], axis=1)),
            "kTh": _cc(np.concatenate([pa["kT"][rows], pb["kT"][rows]], axis=1)),
            "Vh": _cc(np.concatenate([pa["V"][:, rows], pb["V"][:, rows]], axis=0)),
            "relT": relT,
        })
    r2 = run(build_launch2(cfg), im)

    w3m = {
        "woT": _cc(inputs["L0_attn_wo"].T), "w1T": _cc(inputs["L0_ffn_w1"].T),
        "w2T": _cc(inputs["L0_ffn_w2"].T), "w3T": _cc(inputs["L0_ffn_w3"].T),
    }
    im = []
    for c, (b, hh) in enumerate(cores):
        ts = slice(hh * T, (hh + 1) * T)
        pa, pb = r2[2 * b], r2[2 * b + 1]
        im.append(dict(
            w3m,
            xT=_cc(x[b].T[:, ts]),
            xnT=r1[c]["xnT"],
            ctxT=_cc(np.concatenate([pa["ctxTh"][:, ts], pb["ctxTh"][:, ts]], axis=0)),
        ))
    r3 = run(build_launch3(cfg), im)

    w4m = {
        "wqT": _cc(inputs["L1_attn_Wq"].T), "wkT": _cc(inputs["L1_attn_Wk"].T),
        "wvT": _cc(inputs["L1_attn_Wv"].T),
        "bqc": _cc(inputs["L1_attn_bq"].reshape(DJ, 128).T),
        "bkc": _cc(inputs["L1_attn_bk"].reshape(KVJ, 128).T),
        "bvr": _cc(inputs["L1_attn_bv"][None, :]),
    }
    im = [dict(w4m, x1T=r3[c]["x1T"],
               trow=np.arange(hh * T, (hh + 1) * T, dtype=np.float32)[None, :])
          for c, (b, hh) in enumerate(cores)]
    r4 = run(build_launch4(cfg), im)

    qg = inputs["L1_attn_q_gain"]
    im = []
    for c, (b, hh) in enumerate(cores):
        qrows = slice(hh * HHq * HD1, (hh + 1) * HHq * HD1)
        krows = slice(hh * KVJh * 128, (hh + 1) * KVJh * 128)
        pa, pb = r4[2 * b], r4[2 * b + 1]
        im.append({
            "qhTh": _cc(np.concatenate([pa["qhT"][qrows], pb["qhT"][qrows]], axis=1)),
            "khTh": _cc(np.concatenate([pa["khT"][krows], pb["khT"][krows]], axis=1)),
            "Vh": _cc(np.concatenate([pa["V1"][:, krows], pb["V1"][:, krows]], axis=0)),
            "qgr": _cc(qg[hh * HHq:(hh + 1) * HHq][None, :]),
        })
    r5 = run(build_launch5(cfg), im)

    w6m = {
        "woT": _cc(inputs["L1_attn_Wo"].T),
        "boc": _cc(inputs["L1_attn_bo"].reshape(DJ, 128).T),
        "gwT": _cc(inputs["L1_ffn_gate_w"].T),
        "w1T": _cc(inputs["L1_ffn_w1"].transpose(0, 2, 1)),
        "w2T": _cc(inputs["L1_ffn_w2"].transpose(0, 2, 1)),
        "w3T": _cc(inputs["L1_ffn_w3"].transpose(0, 2, 1)),
        "ws1T": _cc(inputs["L1_ffn_ws1"].T), "ws2T": _cc(inputs["L1_ffn_ws2"].T),
        "ws3T": _cc(inputs["L1_ffn_ws3"].T),
    }
    im = []
    for c, (b, hh) in enumerate(cores):
        ts = slice(hh * T, (hh + 1) * T)
        pa, pb = r5[2 * b], r5[2 * b + 1]
        im.append(dict(
            w6m, x1T=r3[c]["x1T"], xn1T=r4[c]["xn1T"],
            ctxT=_cc(np.concatenate([pa["ctxTh"][:, ts], pb["ctxTh"][:, ts]], axis=0)),
        ))
    r6 = run(build_launch6(cfg), im)

    out = np.empty((B, S, D), dtype=np.float32)
    for c, (b, hh) in enumerate(cores):
        out[b, hh * T:(hh + 1) * T, :] = r6[c]["outT"].T
    return out


# version 06
# speedup vs baseline: 1.2818x; 1.2818x over previous
import contextlib
import math
import numpy as np
import concourse.bass as bass
import concourse.mybir as mybir
from concourse.bass_utils import run_bass_kernel_spmd

F32 = mybir.dt.float32
BF16 = mybir.dt.bfloat16
AF = mybir.ActivationFunctionType
ALU = mybir.AluOpType
AX = mybir.AxisListType

ENGINES = ("sync", "scalar", "vector", "tensor", "gpsimd")
DMA_QUEUES = ("sync", "scalar", "gpsimd")

CFG_FULL = dict(D=2048, B=4, S=4096, L0_HEADS=32, L0_HD=64, MAX_POS=512,
                FFN_H=5632, L1_H=16, L1_KVH=4, L1_HD=128, INTER=1024, E=8,
                BASE=10000.0, RMS_EPS=1.1920929e-07, LN_EPS=1e-5)


class Seq:

    def __init__(self, nc):
        self.nc = nc
        self._stack = contextlib.ExitStack()
        self.esem = {}
        self.chsem = {}
        for e in ENGINES:
            self.esem[e] = [self._stack.enter_context(nc.semaphore("se_" + e)), 0]
        self.last_writer = {}
        self.readers = {}
        self.steps = {e: [] for e in ENGINES}
        self.n_steps = 0

    def _ch(self, ch):
        if ch not in self.chsem:
            self.chsem[ch] = [self._stack.enter_context(self.nc.semaphore("ch_" + ch)), 0]
        return self.chsem[ch]

    def step(self, engine, fn, reads=(), writes=(), dma=False, n_dma=1, ch=None):
        waits = {}

        def need(sv):
            if sv is None:
                return
            h, v = sv
            k = id(h)
            if k not in waits or waits[k][1] < v:
                waits[k] = (h, v)

        for t in reads:
            need(self.last_writer.get(t))
        for t in writes:
            need(self.last_writer.get(t))
            for sv in self.readers.get(t, ()):
                need(sv)

        if dma:
            s = self._ch(engine + ":" + ch)
            s[1] += 16 * n_dma
        else:
            s = self.esem[engine]
            s[1] += 1
        marker = (s[0], s[1])
        self.steps[engine].append((fn, list(waits.values()), s[0], dma))
        self.n_steps += 1
        for t in writes:
            self.last_writer[t] = marker
            self.readers[t] = []
        for t in reads:
            self.readers.setdefault(t, []).append(marker)
        return marker

    def emit(self, block):
        for engine in ENGINES:
            steps = self.steps[engine]
            if not steps:
                continue

            @getattr(block, engine)
            def _(eng, steps=steps, engine=engine):
                seen = {}
                for fn, waits, sem, is_dma in steps:
                    for h, v in waits:
                        k = id(h)
                        if seen.get(k, -1) >= v:
                            continue
                        eng.wait_ge(h, v)
                        seen[k] = v
                    if is_dma:
                        fn(eng, lambda inst: inst.then_inc(sem, 16))
                    else:
                        inst = fn(eng)
                        inst.then_inc(sem, 1)

    def fence(self):
        targets = [(s[0], s[1]) for s in self.chsem.values() if s[1] > 0]
        esnap = {e: (s[0], s[1]) for e, s in self.esem.items()}
        for e in ENGINES:
            others = [esnap[o] for o in ENGINES if o != e and esnap[o][1] > 0]

            def fn(eng, others=others, targets=targets):
                eng.drain()
                for h, v in targets:
                    eng.wait_ge(h, v)
                for h, v in others:
                    eng.wait_ge(h, v)
                return eng.drain()
            self.step(e, fn)

    def add_final_wait(self):
        targets = [(q, s[0], s[1]) for q, s in self.chsem.items() if s[1] > 0]
        esems = [(e, self.esem[e][0], self.esem[e][1]) for e in ENGINES
                 if self.esem[e][1] > 0]

        def fn(eng):
            for _, h, v in targets:
                eng.wait_ge(h, v)
            for name, h, v in esems:
                if name != "sync":
                    eng.wait_ge(h, v)
            return eng.drain()

        s = self.esem["sync"]
        s[1] += 1
        self.steps["sync"].append((fn, [], s[0], False))


def _dj_view(ap, t0, tc):
    return ap.rearrange("(j p) t -> p j t", p=128)[:, :, t0:t0 + tc]


def load_cast(sq, queue, dst_ap, src_ap, reads, writes, ch):
    def fn(e, mark):
        mark(e.dma_start(dst_ap, src_ap))
    sq.step(queue, fn, reads=reads, writes=writes, dma=True, n_dma=1, ch=ch)


def store(sq, queue, dst_ap, src_ap, reads, writes, ch):
    def fn(e, mark):
        mark(e.dma_start(dst_ap, src_ap))
    sq.step(queue, fn, reads=reads, writes=writes, dma=True, n_dma=1, ch=ch)


def build_launch1(cfg):
    D, T = cfg["D"], cfg["T"]
    TC = min(512, T)
    DJ, NT = D // 128, T // TC
    EPS = cfg["LN_EPS"]
    nc = bass.Bass("TRN2", target_bir_lowering=False)
    xT = nc.dram_tensor("xT", [D, T], F32, kind="ExternalInput").ap()
    wqT = nc.dram_tensor("wqT", [D, D], F32, kind="ExternalInput").ap()
    wkT = nc.dram_tensor("wkT", [D, D], F32, kind="ExternalInput").ap()
    wvT = nc.dram_tensor("wvT", [D, D], F32, kind="ExternalInput").ap()
    xnT = nc.dram_tensor("xnT", [D, T], BF16, kind="ExternalOutput").ap()
    qT = nc.dram_tensor("qT", [D, T], BF16, kind="ExternalOutput").ap()
    kT = nc.dram_tensor("kT", [D, T], BF16, kind="ExternalOutput").ap()
    V = nc.dram_tensor("V", [T, D], BF16, kind="ExternalOutput").ap()

    sq = Seq(nc)
    with contextlib.ExitStack() as es:
        sb = lambda name, shape, dt: es.enter_context(nc.sbuf_tensor(name, shape, dt))
        ps = lambda name, shape, dt=F32: es.enter_context(nc.psum_tensor(name, shape, dt))
        ones_bf = sb("ones_bf", [128, 1], BF16)
        ones_f32r = sb("ones_f32r", [1, 128], F32)
        xc = sb("xc", [128, DJ, TC], F32)
        xcb = sb("xcb", [128, 2, DJ, TC], BF16)
        xsqb = sb("xsqb", [128, DJ, TC], BF16)
        xnb = sb("xnb", [128, 2, DJ, TC], BF16)
        mrow = sb("mrow", [1, TC], F32)
        rrow = sb("rrow", [1, TC], F32)
        w_sb = sb("w_sb", [128, DJ, D], BF16)
        ob = sb("ob", [128, 4, 512], BF16)
        ps_sum = ps("ps_sum", [1, 512])
        ps_sumsq = ps("ps_sumsq", [1, 512])
        ps_mbc = ps("ps_mbc", [128, 512])
        ps_rbc = ps("ps_rbc", [128, 512])
        ps_o = ps("ps_o", [128, 4, 512])

        sq.step("gpsimd", lambda e: e.memset(ones_bf[:], 1.0), writes=[("ones_bf",)])
        sq.step("gpsimd", lambda e: e.memset(ones_f32r[:], 1.0), writes=[("ones_f32r",)])

        for it in range(NT):
            t0 = it * TC
            sl = it % 2
            load_cast(sq, "sync", xc[:], _dj_view(xT, t0, TC),
                      reads=[], writes=[("xc",)], ch="xc")
            load_cast(sq, "gpsimd", xcb[:, sl], _dj_view(xT, t0, TC),
                      reads=[], writes=[("xcb", sl)], ch="xcb%d" % sl)
            sq.step("scalar", lambda e, sl=sl: e.activation(xsqb[:], xcb[:, sl], AF.Square),
                    reads=[("xcb", sl)], writes=[("xsqb",)])

            def mm_sums(e, sl=sl):
                for dj in range(DJ):
                    last = e.matmul(ps_sum[:, 0:TC], ones_bf[:], xcb[:, sl, dj],
                                    start=(dj == 0), stop=(dj == DJ - 1))
                return last
            sq.step("tensor", mm_sums, reads=[("xcb", sl), ("ones_bf",)],
                    writes=[("ps_sum",)])

            def mm_sumsq(e):
                for dj in range(DJ):
                    last = e.matmul(ps_sumsq[:, 0:TC], ones_bf[:], xsqb[:, dj],
                                    start=(dj == 0), stop=(dj == DJ - 1))
                return last
            sq.step("tensor", mm_sumsq, reads=[("xsqb",), ("ones_bf",)],
                    writes=[("ps_sumsq",)])

            sq.step("vector", lambda e: e.tensor_scalar(mrow[:], ps_sum[:, 0:TC], 1.0 / D, None, ALU.mult),
                    reads=[("ps_sum",)], writes=[("mrow",)])
            sq.step("vector", lambda e: e.tensor_tensor(rrow[:], mrow[:], mrow[:], ALU.mult),
                    reads=[("mrow",)], writes=[("rrow",)])
            sq.step("vector", lambda e: e.scalar_tensor_tensor(rrow[:], ps_sumsq[:, 0:TC], 1.0 / D, rrow[:], ALU.mult, ALU.subtract),
                    reads=[("ps_sumsq",), ("rrow",)], writes=[("rrow",)])
            sq.step("vector", lambda e: e.tensor_scalar(rrow[:], rrow[:], float(EPS), None, ALU.add),
                    reads=[("rrow",)], writes=[("rrow",)])
            sq.step("scalar", lambda e: e.sqrt(rrow[:], rrow[:]),
                    reads=[("rrow",)], writes=[("rrow",)])
            sq.step("vector", lambda e: e.reciprocal(rrow[:], rrow[:]),
                    reads=[("rrow",)], writes=[("rrow",)])
            sq.step("tensor", lambda e: e.matmul(ps_mbc[:, 0:TC], ones_f32r[:], mrow[:], start=True, stop=True),
                    reads=[("mrow",), ("ones_f32r",)], writes=[("ps_mbc",)])
            sq.step("tensor", lambda e: e.matmul(ps_rbc[:, 0:TC], ones_f32r[:], rrow[:], start=True, stop=True),
                    reads=[("rrow",), ("ones_f32r",)], writes=[("ps_rbc",)])
            def norm1(e):
                for dj in range(DJ):
                    last = e.tensor_tensor(xc[:, dj], xc[:, dj], ps_mbc[:, 0:TC], ALU.subtract)
                return last
            sq.step("vector", norm1, reads=[("xc",), ("ps_mbc",)], writes=[("xc",)])

            def norm2(e, sl=sl):
                for dj in range(DJ):
                    last = e.tensor_tensor(xnb[:, sl, dj], xc[:, dj], ps_rbc[:, 0:TC], ALU.mult)
                return last
            sq.step("vector", norm2, reads=[("xc",), ("ps_rbc",)], writes=[("xnb", sl)])
            store(sq, "scalar", _dj_view(xnT, t0, TC), xnb[:, sl],
                  reads=[("xnb", sl)], writes=[("hbm:xnT",)], ch="st_xnT")

        for wap, outap, wtok in ((wqT, qT, "wq"), (wkT, kT, "wk")):
            load_cast(sq, "gpsimd", w_sb[:],
                      wap.rearrange("(j p) e -> p j e", p=128),
                      reads=[], writes=[("w_sb",)], ch="w_sb")
            for it in range(NT):
                t0 = it * TC
                sl = it % 2
                load_cast(sq, "sync", xcb[:, sl], _dj_view(xnT, t0, TC),
                          reads=[("hbm:xnT",)], writes=[("xcb", sl)], ch="xcb%d" % sl)
                for ec in range(DJ):
                    bank = ec % 4

                    def mm(e, sl=sl, ec=ec, bank=bank):
                        for dj in range(DJ):
                            last = e.matmul(ps_o[:, bank, 0:TC], w_sb[:, dj, ec * 128:(ec + 1) * 128],
                                            xcb[:, sl, dj], start=(dj == 0), stop=(dj == DJ - 1))
                        return last
                    sq.step("tensor", mm, reads=[("w_sb",), ("xcb", sl)],
                            writes=[("ps_o", bank)])
                    sq.step("vector", lambda e, bank=bank: e.tensor_copy(ob[:, bank, 0:TC], ps_o[:, bank, 0:TC]),
                            reads=[("ps_o", bank)], writes=[("ob", bank)])
                    store(sq, "scalar", outap[ec * 128:(ec + 1) * 128, t0:t0 + TC], ob[:, bank, 0:TC],
                          reads=[("ob", bank)], writes=[("hbm:" + wtok,)], ch="st_" + wtok)

        load_cast(sq, "gpsimd", w_sb[:],
                  wvT.rearrange("(j p) e -> p j e", p=128),
                  reads=[], writes=[("w_sb",)], ch="w_sb")
        NEC = D // 512 if D >= 512 else 1
        ECW = min(512, D)
        for it in range(NT):
            t0 = it * TC
            sl = it % 2
            load_cast(sq, "sync", xcb[:, sl], _dj_view(xnT, t0, TC),
                      reads=[("hbm:xnT",)], writes=[("xcb", sl)], ch="xcb%d" % sl)
            for tt in range(TC // 128):
                for ec in range(NEC):
                    bank = ec % 4

                    def mm(e, sl=sl, tt=tt, ec=ec, bank=bank):
                        for dj in range(DJ):
                            last = e.matmul(ps_o[:, bank, 0:ECW],
                                            xcb[:, sl, dj, tt * 128:(tt + 1) * 128],
                                            w_sb[:, dj, ec * ECW:(ec + 1) * ECW],
                                            start=(dj == 0), stop=(dj == DJ - 1))
                        return last
                    sq.step("tensor", mm, reads=[("w_sb",), ("xcb", sl)],
                            writes=[("ps_o", bank)])
                    sq.step("vector", lambda e, bank=bank: e.tensor_copy(ob[:, bank, 0:ECW], ps_o[:, bank, 0:ECW]),
                            reads=[("ps_o", bank)], writes=[("ob", bank)])
                    store(sq, "scalar", V[t0 + tt * 128:t0 + tt * 128 + 128, ec * ECW:(ec + 1) * ECW],
                          ob[:, bank, 0:ECW], reads=[("ob", bank)], writes=[("hbm:V",)], ch="st_V")

        sq.add_final_wait()
        with nc.Block() as block:
            sq.emit(block)
    return nc



def _band_deltas(MP, QC):
    lo, hi = -(MP + 126), MP + QC - 2
    d0 = -((-lo - 1) // 128) * 128
    return [d for d in range(d0, hi, 128) if lo < d < hi]


def build_launch2(cfg):
    D, S = cfg["D"], cfg["S"]
    HH = cfg["L0_HEADS"] // 2
    HD = cfg["L0_HD"]
    MP = cfg["MAX_POS"]
    TABLE = 2 * MP - 1
    QC = min(512, S)
    NQC = S // QC
    KT = S // 128
    ISC = 1.0 / math.sqrt(HD)
    deltas = _band_deltas(MP, QC)
    NB = len(deltas)
    dmap = {d: i for i, d in enumerate(deltas)}
    GRP = 3

    nc = bass.Bass("TRN2", target_bir_lowering=False)
    qTh = nc.dram_tensor("qTh", [HH * HD, S], BF16, kind="ExternalInput").ap()
    kTh = nc.dram_tensor("kTh", [HH * HD, S], BF16, kind="ExternalInput").ap()
    Vh = nc.dram_tensor("Vh", [S, HH * HD], BF16, kind="ExternalInput").ap()
    relT = nc.dram_tensor("relT", [HD, TABLE], F32, kind="ExternalInput").ap()
    ctxTh = nc.dram_tensor("ctxTh", [HH * HD, S], BF16, kind="ExternalOutput").ap()
    PAD = 512
    mrev = nc.dram_tensor("mrev", [2 * S - 1 + 2 * PAD], F32, kind="Internal").ap()

    sq = Seq(nc)
    with contextlib.ExitStack() as es:
        sb = lambda name, shape, dt: es.enter_context(nc.sbuf_tensor(name, shape, dt))
        ps = lambda name, shape, dt=F32: es.enter_context(nc.psum_tensor(name, shape, dt))
        ones_bf = sb("ones_bf", [128, 1], BF16)
        ones_f32r = sb("ones_f32r", [1, 128], F32)
        ones_f32c = sb("ones_f32c", [128, 1], F32)
        relb = sb("relb", [HD, TABLE], F32)
        m_row = sb("m_row", [1, TABLE], F32)
        ms_row = sb("ms_row", [1, TABLE], F32)
        crow = sb("crow", [1, 512], F32)
        m0c = sb("m0c", [128, 1], F32)
        mLc = sb("mLc", [128, 1], F32)
        bias_sb = sb("bias_sb", [128, max(NB, 1), QC], F32)
        kths = sb("kths", [HD, 2, S], BF16)
        qths = sb("qths", [HD, 2, S], BF16)
        vhs = sb("vhs", [128, 2, KT, HD], BF16)
        Et = sb("Et", [128, 2 * GRP, 512], BF16)
        rec_row = sb("rec_row", [1, 512], F32)
        bc_sb = sb("bc_sb", [128, 512], F32)
        ob = sb("ob", [128, 2, 512], BF16)
        ps_s = ps("ps_s", [128, 2, GRP, 512])
        ps_ctx = ps("ps_ctx", [128, 512])
        ps_misc = ps("ps_misc", [128, 512])

        sq.step("gpsimd", lambda e: e.memset(ones_bf[:], 1.0), writes=[("ones_bf",)])
        sq.step("gpsimd", lambda e: e.memset(ones_f32r[:], 1.0), writes=[("ones_f32r",)])
        sq.step("gpsimd", lambda e: e.memset(ones_f32c[:], 1.0), writes=[("ones_f32c",)])

        load_cast(sq, "sync", relb[:], relT[:], reads=[], writes=[("relb",)], ch="relb")
        nmc = (TABLE + 511) // 512
        for i in range(nmc):
            c0 = i * 512
            cw = min(512, TABLE - c0)

            def mmm(e, c0=c0, cw=cw):
                return e.matmul(ps_misc[0:1, 0:cw], ones_f32c[0:HD, :], relb[:, c0:c0 + cw],
                                start=True, stop=True)
            sq.step("tensor", mmm, reads=[("relb",), ("ones_f32c",)], writes=[("ps_misc",)])
            sq.step("vector", lambda e, c0=c0, cw=cw: e.tensor_scalar(
                m_row[:, c0:c0 + cw], ps_misc[0:1, 0:cw], 1.0 / HD, None, ALU.mult),
                reads=[("ps_misc",)], writes=[("m_row",)])
            sq.step("vector", lambda e, c0=c0, cw=cw: e.tensor_scalar(
                ms_row[:, c0:c0 + cw], ps_misc[0:1, 0:cw], math.sqrt(HD) / HD, None, ALU.mult),
                reads=[("ps_misc",)], writes=[("ms_row",)])
        sq.step("tensor", lambda e: e.matmul(ps_misc[:, 0:1], ones_f32r[:], m_row[0:1, 0:1],
                                             start=True, stop=True),
                reads=[("m_row",), ("ones_f32r",)], writes=[("ps_misc",)])
        sq.step("vector", lambda e: e.tensor_copy(m0c[:], ps_misc[:, 0:1]),
                reads=[("ps_misc",)], writes=[("m0c",)])
        sq.step("tensor", lambda e: e.matmul(ps_misc[:, 0:1], ones_f32r[:],
                                             m_row[0:1, TABLE - 1:TABLE], start=True, stop=True),
                reads=[("m_row",), ("ones_f32r",)], writes=[("ps_misc",)])
        sq.step("vector", lambda e: e.tensor_copy(mLc[:], ps_misc[:, 0:1]),
                reads=[("ps_misc",)], writes=[("mLc",)])
        sq.step("gpsimd", lambda e: e.memset(crow[:], 1.0), writes=[("crow",)])
        sq.step("vector", lambda e: e.tensor_scalar(
            crow[:], crow[:], ms_row[0:1, 0:1], None, ALU.mult),
            reads=[("ms_row",), ("crow",)], writes=[("crow",)])
        nfill = PAD + S - MP
        off = 0
        while off < nfill:
            w = min(512, nfill - off)
            store(sq, "gpsimd", mrev[off:off + w], crow[0:1, 0:w],
                  reads=[("crow",)], writes=[("hbm:mrev",)], ch="st_mrev")
            off += w
        store(sq, "gpsimd", mrev[PAD + S - MP:PAD + S + MP - 1], ms_row[0:1, :],
              reads=[("ms_row",)], writes=[("hbm:mrev",)], ch="st_mrev")
        sq.step("gpsimd", lambda e: e.memset(crow[:], 1.0), reads=[("crow",)], writes=[("crow",)])
        sq.step("vector", lambda e: e.tensor_scalar(
            crow[:], crow[:], ms_row[0:1, TABLE - 1:TABLE], None, ALU.mult),
            reads=[("ms_row",), ("crow",)], writes=[("crow",)])
        off = PAD + S + MP - 1
        while off < 2 * S - 1 + 2 * PAD:
            w = min(512, 2 * S - 1 + 2 * PAD - off)
            store(sq, "gpsimd", mrev[off:off + w], crow[0:1, 0:w],
                  reads=[("crow",)], writes=[("hbm:mrev",)], ch="st_mrev")
            off += w
        for i, d in enumerate(deltas):
            def ldb(e, mark, i=i, d=d):
                srcap = bass.AP(mrev.tensor, PAD + S + d - QC, [[1, 128], [1, QC]])
                mark(e.dma_start(bias_sb[:, i], srcap))
            sq.step("sync", ldb, reads=[("hbm:mrev",)], writes=[("bias_sb",)],
                    dma=True, n_dma=1, ch="biast")

        for h in range(HH):
            hs = h % 2
            load_cast(sq, "sync", kths[:, hs], kTh[h * HD:(h + 1) * HD, :],
                      reads=[], writes=[("kths", hs)], ch="kths%d" % hs)
            load_cast(sq, "sync", qths[:, hs], qTh[h * HD:(h + 1) * HD, :],
                      reads=[], writes=[("qths", hs)], ch="qths%d" % hs)
            load_cast(sq, "sync", vhs[:, hs],
                      Vh.rearrange("(kt p) e -> p kt e", p=128)[:, :, h * HD:(h + 1) * HD],
                      reads=[], writes=[("vhs", hs)], ch="vhs%d" % hs)
            for qc in range(NQC):
                q0 = qc * QC
                left, band, right = [], [], []
                for kt in range(KT):
                    dlt = kt * 128 - q0
                    if dlt in dmap:
                        band.append(kt)
                    elif dlt < 0:
                        left.append(kt)
                    else:
                        right.append(kt)
                groups = []
                for cls, lst, bias in (("L", left, m0c), ("B", band, None), ("R", right, mLc)):
                    for i in range(0, len(lst), GRP):
                        groups.append((cls, lst[i:i + GRP], bias))
                n_av = 0
                total = KT

                def em_scores(gi, cls, ts, biasc):
                    gs = gi % 2
                    n = len(ts)

                    def mm_s(e, ts=ts, gs=gs, hs=hs, q0=q0):
                        for i, kt in enumerate(ts):
                            last = e.matmul(ps_s[:, gs, i, 0:QC],
                                            kths[:, hs, kt * 128:(kt + 1) * 128],
                                            qths[:, hs, q0:q0 + QC],
                                            start=True, stop=True)
                        return last
                    sq.step("tensor", mm_s, reads=[("kths", hs), ("qths", hs)],
                            writes=[("ps_s", gs)])
                    if cls == "B":
                        def badd(e, ts=ts, gs=gs, q0=q0):
                            for i, kt in enumerate(ts):
                                bi = dmap[kt * 128 - q0]
                                last = e.tensor_tensor(ps_s[:, gs, i, 0:QC], ps_s[:, gs, i, 0:QC],
                                                       bias_sb[:, bi, ::-1], ALU.add)
                            return last
                        sq.step("vector", badd, reads=[("ps_s", gs), ("bias_sb",)],
                                writes=[("ps_s", gs)])

                    def expg(e, n=n, gs=gs, biasc=biasc):
                        bias = 0.0 if biasc is None else biasc[:]
                        return e.activation(Et[:, gs * GRP:gs * GRP + n, 0:QC],
                                            ps_s[:, gs, 0:n, 0:QC], AF.Exp,
                                            bias=bias, scale=ISC)
                    rd = [("ps_s", gs)] + ([] if biasc is None else [(("m0c",) if biasc is m0c else ("mLc",))])
                    sq.step("scalar", expg, reads=rd, writes=[("Et", gs)])

                def em_avsums(gi, ts, n_av):
                    gs = gi % 2

                    def av(e, ts=ts, gs=gs, hs=hs, n_av=n_av, total=total):
                        for i, kt in enumerate(ts):
                            last = e.matmul(ps_ctx[0:HD, 0:QC], vhs[:, hs, kt],
                                            Et[:, gs * GRP + i, 0:QC],
                                            start=(n_av + i == 0), stop=(n_av + i == total - 1))
                        return last
                    sq.step("tensor", av, reads=[("Et", gs), ("vhs", hs)], writes=[("ps_ctx",)])

                    def sums(e, ts=ts, gs=gs, n_av=n_av, total=total):
                        for i, kt in enumerate(ts):
                            last = e.matmul(ps_misc[0:1, 0:QC], ones_bf[:],
                                            Et[:, gs * GRP + i, 0:QC],
                                            start=(n_av + i == 0), stop=(n_av + i == total - 1))
                        return last
                    sq.step("tensor", sums, reads=[("Et", gs), ("ones_bf",)], writes=[("ps_misc",)])

                for gi, (cls, ts, biasc) in enumerate(groups):
                    em_scores(gi, cls, ts, biasc)
                    if gi > 0:
                        pts = groups[gi - 1][1]
                        em_avsums(gi - 1, pts, n_av)
                        n_av += len(pts)
                em_avsums(len(groups) - 1, groups[-1][1], n_av)

                sq.step("vector", lambda e: e.reciprocal(rec_row[0:1, 0:QC], ps_misc[0:1, 0:QC]),
                        reads=[("ps_misc",)], writes=[("rec_row",)])
                sq.step("tensor", lambda e: e.matmul(ps_misc[:, 0:QC], ones_f32r[:],
                                                     rec_row[0:1, 0:QC], start=True, stop=True),
                        reads=[("rec_row",), ("ones_f32r",)], writes=[("ps_misc",)])
                sq.step("vector", lambda e: e.tensor_copy(bc_sb[:, 0:QC], ps_misc[:, 0:QC]),
                        reads=[("ps_misc",)], writes=[("bc_sb",)])
                obs = qc % 2
                sq.step("vector", lambda e, obs=obs: e.tensor_tensor(
                    ob[0:HD, obs, 0:QC], ps_ctx[0:HD, 0:QC], bc_sb[0:HD, 0:QC], ALU.mult),
                    reads=[("ps_ctx",), ("bc_sb",)], writes=[("ob", obs)])
                store(sq, "scalar", ctxTh[h * HD:(h + 1) * HD, q0:q0 + QC], ob[0:HD, obs, 0:QC],
                      reads=[("ob", obs)], writes=[("hbm:ctx",)], ch="st_ctx")

        sq.add_final_wait()
        with nc.Block() as block:
            sq.emit(block)
    return nc



def _emit_gamma(sq, nc, wap, shape, bufs, name):
    R, C = shape
    RJ = R // 128
    wfp, acc, tmpc = bufs["gfp"], bufs["gacc"], bufs["gtmp"]
    ones_f32c, ones_f32r = bufs["ones_f32c"], bufs["ones_f32r"]
    ps_row = bufs["ps_misc"]
    tcol, tncol = bufs[name + "_t"], bufs[name + "_tn"]
    sq.step("gpsimd", lambda e: e.memset(acc[:], 0.0), reads=[("gacc",)], writes=[("gacc",)])
    wv = wap.rearrange("(j p) h -> p j h", p=128)
    for j in range(RJ):
        sl = j % 2
        load_cast(sq, "sync", wfp[:, sl, 0:C], wv[:, j], reads=[], writes=[("gfp", sl)],
                  ch="gfp%d" % sl)
        sq.step("vector", lambda e, sl=sl: e.tensor_reduce(
            tmpc[:], wfp[:, sl, 0:C], AX.X, ALU.add, apply_absolute_value=True),
            reads=[("gfp", sl)], writes=[("gtmp",)])
        sq.step("vector", lambda e: e.tensor_tensor(acc[:], acc[:], tmpc[:], ALU.add),
                reads=[("gtmp",), ("gacc",)], writes=[("gacc",)])
    sq.step("tensor", lambda e: e.matmul(ps_row[0:1, 0:1], acc[:], ones_f32c[:],
                                         start=True, stop=True),
            reads=[("gacc",), ("ones_f32c",)], writes=[("ps_misc",)])
    sq.step("vector", lambda e: e.tensor_scalar(
        tmpc[0:1, 0:1], ps_row[0:1, 0:1], 0.5 / (R * C), 0.5e-8, ALU.mult, ALU.add),
        reads=[("ps_misc",)], writes=[("gtmp",)])
    sq.step("tensor", lambda e: e.matmul(ps_row[:, 0:1], ones_f32r[:], tmpc[0:1, 0:1],
                                         start=True, stop=True),
            reads=[("gtmp",), ("ones_f32r",)], writes=[("ps_misc",)])
    sq.step("vector", lambda e: e.tensor_copy(tcol[:], ps_row[:, 0:1]),
            reads=[("ps_misc",)], writes=[(name + "_t",)])
    sq.step("vector", lambda e: e.tensor_scalar(tncol[:], tcol[:], -1.0, None, ALU.mult),
            reads=[(name + "_t",)], writes=[(name + "_tn",)])


def _emit_quant(sq, wq_ap, wfp_ap, s1_ap, s2_ap, tcol, tncol, toks, name_t, name_tn):
    rfp, w_s1, w_s2, w_q = toks
    sq.step("scalar", lambda e: e.activation(s1_ap, wfp_ap, AF.Sign, bias=tncol[:]),
            reads=[rfp, (name_tn,)], writes=[w_s1])
    sq.step("scalar", lambda e: e.activation(s2_ap, wfp_ap, AF.Sign, bias=tcol[:]),
            reads=[rfp, (name_t,)], writes=[w_s2])
    sq.step("vector", lambda e: e.tensor_tensor(wq_ap, s1_ap, s2_ap, ALU.add),
            reads=[w_s1, w_s2], writes=[w_q])


def build_launch3(cfg):
    D, T = cfg["D"], cfg["T"]
    H = cfg["FFN_H"]
    TC = min(512, T)
    DJ, NT = D // 128, T // TC
    HJ = H // 128
    nc = bass.Bass("TRN2", target_bir_lowering=False)
    xT = nc.dram_tensor("xT", [D, T], F32, kind="ExternalInput").ap()
    xnT = nc.dram_tensor("xnT", [D, T], BF16, kind="ExternalInput").ap()
    ctxT = nc.dram_tensor("ctxT", [D, T], BF16, kind="ExternalInput").ap()
    woT = nc.dram_tensor("woT", [D, D], F32, kind="ExternalInput").ap()
    w1T = nc.dram_tensor("w1T", [D, H], F32, kind="ExternalInput").ap()
    w2T = nc.dram_tensor("w2T", [D, H], F32, kind="ExternalInput").ap()
    w3T = nc.dram_tensor("w3T", [H, D], F32, kind="ExternalInput").ap()
    x1T = nc.dram_tensor("x1T", [D, T], F32, kind="ExternalOutput").ap()
    hidT = nc.dram_tensor("hidT", [H, T], BF16, kind="Internal").ap()
    ffnT = nc.dram_tensor("ffnT", [D, T], BF16, kind="Internal").ap()

    sq = Seq(nc)
    with contextlib.ExitStack() as es:
        sb = lambda name, shape, dt: es.enter_context(nc.sbuf_tensor(name, shape, dt))
        ps = lambda name, shape, dt=F32: es.enter_context(nc.psum_tensor(name, shape, dt))
        ones_f32c = sb("ones_f32c", [128, 1], F32)
        ones_f32r = sb("ones_f32r", [1, 128], F32)
        bufs = {"ones_f32c": ones_f32c, "ones_f32r": ones_f32r}
        for nm in ("w1", "w2", "w3"):
            bufs[nm + "_t"] = sb(nm + "_t", [128, 1], F32)
            bufs[nm + "_tn"] = sb(nm + "_tn", [128, 1], F32)
        ps_misc = ps("ps_misc", [128, 512])
        bufs["ps_misc"] = ps_misc
        sq.step("gpsimd", lambda e: e.memset(ones_f32c[:], 1.0), writes=[("ones_f32c",)])
        sq.step("gpsimd", lambda e: e.memset(ones_f32r[:], 1.0), writes=[("ones_f32r",)])

        def _stageA():
          with contextlib.ExitStack() as st_es:
            bufs["gfp"] = st_es.enter_context(nc.sbuf_tensor("gfp", [128, 2, H], F32))
            bufs["gacc"] = st_es.enter_context(nc.sbuf_tensor("gacc", [128, 1], F32))
            bufs["gtmp"] = st_es.enter_context(nc.sbuf_tensor("gtmp", [128, 1], F32))
            _emit_gamma(sq, nc, w1T, (D, H), bufs, "w1")
            _emit_gamma(sq, nc, w2T, (D, H), bufs, "w2")
            _emit_gamma(sq, nc, w3T, (H, D), bufs, "w3")
            sq.fence()
        _stageA()

        HC = min(512, H)
        NHC = H // HC
        def _stageB():
          with contextlib.ExitStack() as st_es:
            sb2 = lambda name, shape, dt: st_es.enter_context(nc.sbuf_tensor(name, shape, dt))
            xnb = sb2("xnb", [128, DJ, T], BF16)
            wfp = sb2("wfp", [128, DJ, HC], F32)
            s1b = sb2("s1b", [128, DJ, HC], BF16)
            s2b = sb2("s2b", [128, DJ, HC], BF16)
            w1q = sb2("w1q", [128, DJ, HC], BF16)
            w2q = sb2("w2q", [128, DJ, HC], BF16)
            sgb = sb2("sgb", [128, 2, 512], BF16)
            hob = sb2("hob", [128, 2, 512], BF16)
            ps_a = st_es.enter_context(nc.psum_tensor("ps_a", [128, 2, 512], F32))
            ps_b = st_es.enter_context(nc.psum_tensor("ps_b", [128, 2, 512], F32))
            load_cast(sq, "gpsimd", xnb[:], xnT.rearrange("(j p) t -> p j t", p=128),
                      reads=[], writes=[("xnb",)], ch="xnb")
            for hc in range(NHC):
                for wtag, wap, wq in (("w1", w1T, w1q), ("w2", w2T, w2q)):
                    load_cast(sq, "sync", wfp[:],
                              wap.rearrange("(j p) h -> p j h", p=128)[:, :, hc * HC:(hc + 1) * HC],
                              reads=[], writes=[("wfp",)], ch="wfp")
                    _emit_quant(sq, wq[:], wfp[:], s1b[:], s2b[:],
                                bufs[wtag + "_t"], bufs[wtag + "_tn"],
                                [("wfp",), ("s1b",), ("s2b",), (wtag + "q",)], wtag + "_t", wtag + "_tn")
                for tc in range(NT):
                    t0 = tc * TC
                    for sub in range(HC // 128):
                        pslot = sub % 2

                        def mma(e, hc=hc, tc=tc, sub=sub, pslot=pslot, t0=t0):
                            for dj in range(DJ):
                                last = e.matmul(ps_a[:, pslot, 0:TC],
                                                w1q[:, dj, sub * 128:(sub + 1) * 128],
                                                xnb[:, dj, t0:t0 + TC],
                                                start=(dj == 0), stop=(dj == DJ - 1))
                            return last
                        sq.step("tensor", mma, reads=[("w1q",), ("xnb",)], writes=[("ps_a", pslot)])

                        def mmb(e, hc=hc, tc=tc, sub=sub, pslot=pslot, t0=t0):
                            for dj in range(DJ):
                                last = e.matmul(ps_b[:, pslot, 0:TC],
                                                w2q[:, dj, sub * 128:(sub + 1) * 128],
                                                xnb[:, dj, t0:t0 + TC],
                                                start=(dj == 0), stop=(dj == DJ - 1))
                            return last
                        sq.step("tensor", mmb, reads=[("w2q",), ("xnb",)], writes=[("ps_b", pslot)])
                        sq.step("scalar", lambda e, pslot=pslot: e.activation(
                            sgb[:, pslot, 0:TC], ps_a[:, pslot, 0:TC], AF.Sigmoid, scale=0.5),
                            reads=[("ps_a", pslot)], writes=[("sgb", pslot)])
                        sq.step("vector", lambda e, pslot=pslot: e.scalar_tensor_tensor(
                            hob[:, pslot, 0:TC], ps_b[:, pslot, 0:TC], 0.5, sgb[:, pslot, 0:TC],
                            ALU.mult, ALU.mult),
                            reads=[("ps_b", pslot), ("sgb", pslot)], writes=[("hob", pslot)])
                        store(sq, "scalar", hidT[hc * HC + sub * 128:hc * HC + (sub + 1) * 128, t0:t0 + TC],
                              hob[:, pslot, 0:TC], reads=[("hob", pslot)],
                              writes=[("hbm:hidT",)], ch="st_hid")
            sq.fence()
        _stageB()

        EC3 = min(512, D)
        NEC3 = D // EC3
        def _stageC():
          with contextlib.ExitStack() as st_es:
            sb2 = lambda name, shape, dt: st_es.enter_context(nc.sbuf_tensor(name, shape, dt))
            w3fp = sb2("w3fp", [128, HJ, 128], F32)
            s1b = sb2("s1c", [128, HJ, 128], BF16)
            s2b = sb2("s2c", [128, HJ, 128], BF16)
            w3q = sb2("w3q", [128, HJ, EC3], BF16)
            hid = sb2("hid", [128, HJ, TC], BF16)
            fob = sb2("fob", [128, 2, 512], BF16)
            ps_a = st_es.enter_context(nc.psum_tensor("ps_a3", [128, 2, 512], F32))
            w3v = w3T.rearrange("(j p) ee -> p j ee", p=128)
            for ec in range(NEC3):
                for s4 in range(EC3 // 128):
                    c0 = ec * EC3 + s4 * 128
                    load_cast(sq, "sync", w3fp[:], w3v[:, :, c0:c0 + 128],
                              reads=[], writes=[("wfp",)], ch="wfp")
                    _emit_quant(sq, w3q[:, :, s4 * 128:(s4 + 1) * 128], w3fp[:], s1b[:], s2b[:],
                                bufs["w3_t"], bufs["w3_tn"],
                                [("wfp",), ("s1b",), ("s2b",), ("w3q", s4)], "w3_t", "w3_tn")
                for tc in range(NT):
                    t0 = tc * TC
                    load_cast(sq, "sync", hid[:],
                              hidT.rearrange("(j p) t -> p j t", p=128)[:, :, t0:t0 + TC],
                              reads=[("hbm:hidT",)], writes=[("hid",)], ch="hid")
                    for sub in range(EC3 // 128):
                        pslot = sub % 2

                        def mma(e, ec=ec, sub=sub, pslot=pslot):
                            for it in range(HJ):
                                last = e.matmul(ps_a[:, pslot, 0:TC],
                                                w3q[:, it, sub * 128:(sub + 1) * 128],
                                                hid[:, it],
                                                start=(it == 0), stop=(it == HJ - 1))
                            return last
                        sq.step("tensor", mma, reads=[("w3q", sub), ("hid",)], writes=[("ps_a", pslot)])
                        sq.step("vector", lambda e, pslot=pslot: e.tensor_scalar(
                            fob[:, pslot, 0:TC], ps_a[:, pslot, 0:TC], 0.5, None, ALU.mult),
                            reads=[("ps_a", pslot)], writes=[("fob", pslot)])
                        store(sq, "scalar", ffnT[ec * EC3 + sub * 128:ec * EC3 + (sub + 1) * 128, t0:t0 + TC],
                              fob[:, pslot, 0:TC], reads=[("fob", pslot)],
                              writes=[("hbm:ffnT",)], ch="st_ffn")
            sq.fence()
        _stageC()

        def _stageD():
          with contextlib.ExitStack() as st_es:
            sb2 = lambda name, shape, dt: st_es.enter_context(nc.sbuf_tensor(name, shape, dt))
            wo_sb = sb2("wo_sb", [128, DJ, D], BF16)
            cxb = sb2("cxb", [128, 2, DJ, TC], BF16)
            xfb = sb2("xfb", [128, 2, TC], F32)
            ffb = sb2("ffb", [128, 2, TC], BF16)
            t1b = sb2("t1b", [128, 2, TC], F32)
            ps_a = st_es.enter_context(nc.psum_tensor("ps_a4", [128, 2, 512], F32))
            load_cast(sq, "gpsimd", wo_sb[:], woT.rearrange("(j p) e -> p j e", p=128),
                      reads=[], writes=[("wo_sb",)], ch="wo_sb")
            for tc in range(NT):
                t0 = tc * TC
                sl = tc % 2
                load_cast(sq, "sync", cxb[:, sl], _dj_view(ctxT, t0, TC),
                          reads=[], writes=[("cxb", sl)], ch="cxb%d" % sl)
                for ec in range(DJ):
                    pslot = ec % 2
                    xsl = ec % 2

                    def mm(e, sl=sl, ec=ec, pslot=pslot):
                        for dj in range(DJ):
                            last = e.matmul(ps_a[:, pslot, 0:TC],
                                            wo_sb[:, dj, ec * 128:(ec + 1) * 128],
                                            cxb[:, sl, dj], start=(dj == 0), stop=(dj == DJ - 1))
                        return last
                    sq.step("tensor", mm, reads=[("wo_sb",), ("cxb", sl)], writes=[("ps_a", pslot)])
                    load_cast(sq, "sync", xfb[:, xsl], xT[ec * 128:(ec + 1) * 128, t0:t0 + TC],
                              reads=[], writes=[("xfb", xsl)], ch="xfb%d" % xsl)
                    load_cast(sq, "sync", ffb[:, xsl], ffnT[ec * 128:(ec + 1) * 128, t0:t0 + TC],
                              reads=[("hbm:ffnT",)], writes=[("ffb", xsl)], ch="ffb%d" % xsl)
                    sq.step("vector", lambda e, pslot=pslot, xsl=xsl: e.tensor_tensor(
                        t1b[:, xsl, 0:TC], ps_a[:, pslot, 0:TC], xfb[:, xsl, 0:TC], ALU.add),
                        reads=[("ps_a", pslot), ("xfb", xsl)], writes=[("t1b", xsl)])
                    sq.step("vector", lambda e, xsl=xsl: e.tensor_tensor(
                        t1b[:, xsl, 0:TC], t1b[:, xsl, 0:TC], ffb[:, xsl, 0:TC], ALU.add),
                        reads=[("t1b", xsl), ("ffb", xsl)], writes=[("t1b", xsl)])
                    store(sq, "scalar", x1T[ec * 128:(ec + 1) * 128, t0:t0 + TC],
                          t1b[:, xsl, 0:TC], reads=[("t1b", xsl)],
                          writes=[("hbm:x1T",)], ch="st_x1")
        _stageD()

        sq.add_final_wait()
        with nc.Block() as block:
            sq.emit(block)
    return nc



def _emit_ln(sq, nc, es, x_ap, xn_ap, D, T, TC, eps, consts):
    DJ, NT = D // 128, T // TC
    ones_bf, ones_f32r = consts["ones_bf"], consts["ones_f32r"]
    sb = lambda name, shape, dt: es.enter_context(nc.sbuf_tensor(name, shape, dt))
    xc = sb("ln_xc", [128, DJ, TC], F32)
    xcb = sb("ln_xcb", [128, 2, DJ, TC], BF16)
    xsqb = sb("ln_xsqb", [128, DJ, TC], BF16)
    xnb = sb("ln_xnb", [128, 2, DJ, TC], BF16)
    mrow = sb("ln_mrow", [1, TC], F32)
    rrow = sb("ln_rrow", [1, TC], F32)
    ps_sum = es.enter_context(nc.psum_tensor("ln_ps_sum", [1, 512], F32))
    ps_sumsq = es.enter_context(nc.psum_tensor("ln_ps_sumsq", [1, 512], F32))
    ps_mbc = es.enter_context(nc.psum_tensor("ln_ps_mbc", [128, 512], F32))
    ps_rbc = es.enter_context(nc.psum_tensor("ln_ps_rbc", [128, 512], F32))
    for it in range(NT):
        t0 = it * TC
        sl = it % 2
        load_cast(sq, "sync", xc[:], _dj_view(x_ap, t0, TC),
                  reads=[], writes=[("ln_xc",)], ch="ln_xc")
        load_cast(sq, "gpsimd", xcb[:, sl], _dj_view(x_ap, t0, TC),
                  reads=[], writes=[("ln_xcb", sl)], ch="ln_xcb%d" % sl)
        sq.step("scalar", lambda e, sl=sl: e.activation(xsqb[:], xcb[:, sl], AF.Square),
                reads=[("ln_xcb", sl)], writes=[("ln_xsqb",)])

        def mm_sums(e, sl=sl):
            for dj in range(DJ):
                last = e.matmul(ps_sum[:, 0:TC], ones_bf[:], xcb[:, sl, dj],
                                start=(dj == 0), stop=(dj == DJ - 1))
            return last
        sq.step("tensor", mm_sums, reads=[("ln_xcb", sl), ("ones_bf",)], writes=[("ln_ps_sum",)])

        def mm_sumsq(e):
            for dj in range(DJ):
                last = e.matmul(ps_sumsq[:, 0:TC], ones_bf[:], xsqb[:, dj],
                                start=(dj == 0), stop=(dj == DJ - 1))
            return last
        sq.step("tensor", mm_sumsq, reads=[("ln_xsqb",), ("ones_bf",)], writes=[("ln_ps_sumsq",)])
        sq.step("vector", lambda e: e.tensor_scalar(mrow[:], ps_sum[0:1, 0:TC], 1.0 / D, None, ALU.mult),
                reads=[("ln_ps_sum",)], writes=[("ln_mrow",)])
        sq.step("vector", lambda e: e.tensor_tensor(rrow[:], mrow[:], mrow[:], ALU.mult),
                reads=[("ln_mrow",)], writes=[("ln_rrow",)])
        sq.step("vector", lambda e: e.scalar_tensor_tensor(rrow[:], ps_sumsq[0:1, 0:TC], 1.0 / D,
                                                           rrow[:], ALU.mult, ALU.subtract),
                reads=[("ln_ps_sumsq",), ("ln_rrow",)], writes=[("ln_rrow",)])
        sq.step("vector", lambda e: e.tensor_scalar(rrow[:], rrow[:], float(eps), None, ALU.add),
                reads=[("ln_rrow",)], writes=[("ln_rrow",)])
        sq.step("scalar", lambda e: e.sqrt(rrow[:], rrow[:]),
                reads=[("ln_rrow",)], writes=[("ln_rrow",)])
        sq.step("vector", lambda e: e.reciprocal(rrow[:], rrow[:]),
                reads=[("ln_rrow",)], writes=[("ln_rrow",)])
        sq.step("tensor", lambda e: e.matmul(ps_mbc[:, 0:TC], ones_f32r[:], mrow[:], start=True, stop=True),
                reads=[("ln_mrow",), ("ones_f32r",)], writes=[("ln_ps_mbc",)])
        sq.step("tensor", lambda e: e.matmul(ps_rbc[:, 0:TC], ones_f32r[:], rrow[:], start=True, stop=True),
                reads=[("ln_rrow",), ("ones_f32r",)], writes=[("ln_ps_rbc",)])

        def norm1(e):
            for dj in range(DJ):
                last = e.tensor_tensor(xc[:, dj], xc[:, dj], ps_mbc[:, 0:TC], ALU.subtract)
            return last
        sq.step("vector", norm1, reads=[("ln_xc",), ("ln_ps_mbc",)], writes=[("ln_xc",)])

        def norm2(e, sl=sl):
            for dj in range(DJ):
                last = e.tensor_tensor(xnb[:, sl, dj], xc[:, dj], ps_rbc[:, 0:TC], ALU.mult)
            return last
        sq.step("vector", norm2, reads=[("ln_xc",), ("ln_ps_rbc",)], writes=[("ln_xnb", sl)])
        store(sq, "scalar", _dj_view(xn_ap, t0, TC), xnb[:, sl],
              reads=[("ln_xnb", sl)], writes=[("hbm:xn",)], ch="st_xn")


def build_launch4(cfg):
    D, T = cfg["D"], cfg["T"]
    H1, KVH = cfg["L1_H"], cfg["L1_KVH"]
    HD = cfg["L1_HD"]
    KVD = KVH * HD
    TC = min(512, T)
    DJ, NT = D // 128, T // TC
    KVJ = KVD // 128
    EPS = cfg["RMS_EPS"]
    BASE = cfg["BASE"]
    nc = bass.Bass("TRN2", target_bir_lowering=False)
    x1T = nc.dram_tensor("x1T", [D, T], F32, kind="ExternalInput").ap()
    wqT = nc.dram_tensor("wqT", [D, D], F32, kind="ExternalInput").ap()
    wkT = nc.dram_tensor("wkT", [D, KVD], F32, kind="ExternalInput").ap()
    wvT = nc.dram_tensor("wvT", [D, KVD], F32, kind="ExternalInput").ap()
    bqc = nc.dram_tensor("bqc", [128, DJ], F32, kind="ExternalInput").ap()
    bkc = nc.dram_tensor("bkc", [128, KVJ], F32, kind="ExternalInput").ap()
    bvr = nc.dram_tensor("bvr", [1, KVD], F32, kind="ExternalInput").ap()
    trow = nc.dram_tensor("trow", [1, T], F32, kind="ExternalInput").ap()
    xn1T = nc.dram_tensor("xn1T", [D, T], BF16, kind="ExternalOutput").ap()
    qhT = nc.dram_tensor("qhT", [D, T], BF16, kind="ExternalOutput").ap()
    khT = nc.dram_tensor("khT", [KVD, T], BF16, kind="ExternalOutput").ap()
    V1 = nc.dram_tensor("V1", [T, KVD], BF16, kind="ExternalOutput").ap()

    sq = Seq(nc)
    with contextlib.ExitStack() as es:
        sb = lambda name, shape, dt: es.enter_context(nc.sbuf_tensor(name, shape, dt))
        ones_bf = sb("ones_bf", [128, 1], BF16)
        ones_f32r = sb("ones_f32r", [1, 128], F32)
        consts = {"ones_bf": ones_bf, "ones_f32r": ones_f32r}
        sq.step("gpsimd", lambda e: e.memset(ones_bf[:], 1.0), writes=[("ones_bf",)])
        sq.step("gpsimd", lambda e: e.memset(ones_f32r[:], 1.0), writes=[("ones_f32r",)])
        Ct = sb("Ct", [128, T], F32)
        St = sb("St", [128, T], F32)
        trows = sb("trows", [1, T], F32)
        invf = sb("invf", [128, 1], F32)
        frq = sb("frq", [128, T], F32)
        mm1 = sb("mm1", [128, 512], F32)
        yb = sb("yb", [128, 512], F32)
        zb = sb("zb", [128, 512], F32)
        bq_sb = sb("bq_sb", [128, DJ], F32)
        bk_sb = sb("bk_sb", [128, KVJ], F32)
        bvbc = sb("bvbc", [128, KVD], F32)

        with contextlib.ExitStack() as st_es:
            _emit_ln(sq, nc, st_es, x1T, xn1T, D, T, TC, cfg["LN_EPS"], consts)
            sq.fence()

        with contextlib.ExitStack() as st_es:
            ps_b = st_es.enter_context(nc.psum_tensor("ps_rope", [128, 512], F32))
            load_cast(sq, "sync", trows[:], trow[:], reads=[], writes=[("trows",)], ch="trows")
            load_cast(sq, "sync", bq_sb[:], bqc[:], reads=[], writes=[("bq_sb",)], ch="bq")
            load_cast(sq, "sync", bk_sb[:], bkc[:], reads=[], writes=[("bk_sb",)], ch="bk")
            sq.step("gpsimd", lambda e: e.iota(invf[:], [[0, 1]], base=0, channel_multiplier=1,
                                              allow_small_or_imprecise_dtypes=True),
                    writes=[("invf",)])
            sq.step("vector", lambda e: e.tensor_scalar(invf[64:128, :], invf[64:128, :],
                                                        -64.0, None, ALU.add),
                    reads=[("invf",)], writes=[("invf",)])
            sq.step("vector", lambda e: e.tensor_scalar(invf[:], invf[:],
                                                        -math.log(BASE) / (HD // 2), None, ALU.mult),
                    reads=[("invf",)], writes=[("invf",)])
            sq.step("scalar", lambda e: e.activation(invf[:], invf[:], AF.Exp),
                    reads=[("invf",)], writes=[("invf",)])
            TPI = 2.0 * math.pi
            for it in range(NT):
                t0 = it * TC
                sq.step("tensor", lambda e, t0=t0: e.matmul(ps_b[:, 0:TC], ones_f32r[:],
                                                            trows[0:1, t0:t0 + TC], start=True, stop=True),
                        reads=[("trows",), ("ones_f32r",)], writes=[("ps_rope",)])
                sq.step("vector", lambda e, t0=t0: e.tensor_scalar(frq[:, t0:t0 + TC], ps_b[:, 0:TC],
                                                                   invf[:], None, ALU.mult),
                        reads=[("ps_rope",), ("invf",)], writes=[("frq", it)])
                for tab, phase, ttok in ((St, 0.0, "St"), (Ct, math.pi / 2.0, "Ct")):
                    dst = tab[:, t0:t0 + TC]
                    sq.step("vector", lambda e, phase=phase, t0=t0: e.tensor_scalar(
                        yb[:, 0:TC], frq[:, t0:t0 + TC], phase, None, ALU.add),
                        reads=[("frq", it)], writes=[("yb",)])
                    sq.step("vector", lambda e: e.tensor_scalar(
                        zb[:, 0:TC], yb[:, 0:TC], 1.0 / TPI, None, ALU.mult),
                        reads=[("yb",)], writes=[("zb",)])
                    sq.step("vector", lambda e: e.tensor_scalar(
                        mm1[:, 0:TC], zb[:, 0:TC], 8388608.0, -8388608.0, ALU.add, ALU.add),
                        reads=[("zb",)], writes=[("mm1",)])
                    sq.step("vector", lambda e: e.tensor_tensor(
                        zb[:, 0:TC], mm1[:, 0:TC], zb[:, 0:TC], ALU.is_gt),
                        reads=[("zb",), ("mm1",)], writes=[("zb",)])
                    sq.step("vector", lambda e: e.scalar_tensor_tensor(
                        mm1[:, 0:TC], zb[:, 0:TC], -1.0, mm1[:, 0:TC], ALU.mult, ALU.add),
                        reads=[("zb",), ("mm1",)], writes=[("mm1",)])
                    sq.step("vector", lambda e, dst=dst: e.scalar_tensor_tensor(
                        dst, mm1[:, 0:TC], -TPI, yb[:, 0:TC], ALU.mult, ALU.add),
                        reads=[("mm1",), ("yb",)], writes=[(ttok + "r", it)])
                    sq.step("vector", lambda e, dst=dst: e.tensor_single_scalar(
                        mm1[:, 0:TC], dst, math.pi, ALU.is_gt),
                        reads=[(ttok + "r", it)], writes=[("mm1",)])
                    sq.step("vector", lambda e, dst=dst: e.scalar_tensor_tensor(
                        dst, mm1[:, 0:TC], -TPI, dst, ALU.mult, ALU.add),
                        reads=[("mm1",), (ttok + "r", it)], writes=[(ttok + "r", it)])
                    sq.step("scalar", lambda e, dst=dst: e.activation(dst, dst, AF.Sin),
                        reads=[(ttok + "r", it)], writes=[(ttok + "r", it)])
            def negs(e):
                return e.tensor_scalar(St[0:64, :], St[0:64, :], -1.0, None, ALU.mult)
            sq.step("vector", negs, reads=[("Str", 0), ("Str", NT - 1)] + [("Str", i) for i in range(NT)],
                    writes=[("St_done",)])
            sq.fence()

        def qk_stage(wap, w_cols, bias_sb, bias_tok, out_ap, nheads, ch_tag):
            with contextlib.ExitStack() as st_es:
                sb2 = lambda name, shape, dt: st_es.enter_context(nc.sbuf_tensor(name, shape, dt))
                w_sb = sb2(ch_tag + "w_sb", [128, DJ, w_cols], BF16)
                xnb = sb2(ch_tag + "xnb", [128, 2, DJ, TC], BF16)
                kf = sb2(ch_tag + "kf", [128, TC], F32)
                sqb = sb2(ch_tag + "sqb", [128, TC], BF16)
                rr = sb2(ch_tag + "rr", [1, TC], F32)
                t1 = sb2(ch_tag + "t1", [128, TC], F32)
                t2 = sb2(ch_tag + "t2", [128, TC], F32)
                obuf = sb2(ch_tag + "ob", [128, 2, TC], BF16)
                ps_p = st_es.enter_context(nc.psum_tensor(ch_tag + "ps_p", [128, 2, 512], F32))
                ps_r = st_es.enter_context(nc.psum_tensor(ch_tag + "ps_r", [1, 512], F32))
                ps_rb = st_es.enter_context(nc.psum_tensor(ch_tag + "ps_rb", [128, 512], F32))
                load_cast(sq, "gpsimd", w_sb[:], wap.rearrange("(j p) e -> p j e", p=128),
                          reads=[], writes=[(ch_tag + "w",)], ch=ch_tag + "w")
                for tc in range(NT):
                    t0 = tc * TC
                    sl = tc % 2
                    load_cast(sq, "sync", xnb[:, sl], _dj_view(xn1T, t0, TC),
                              reads=[("hbm:xn",)], writes=[(ch_tag + "xnb", sl)], ch=ch_tag + "x%d" % sl)
                    for hh in range(nheads):
                        pslot = hh % 2

                        def mm(e, sl=sl, hh=hh, pslot=pslot):
                            for dj in range(DJ):
                                last = e.matmul(ps_p[:, pslot, 0:TC],
                                                w_sb[:, dj, hh * 128:(hh + 1) * 128],
                                                xnb[:, sl, dj], start=(dj == 0), stop=(dj == DJ - 1))
                            return last
                        sq.step("tensor", mm, reads=[(ch_tag + "w",), (ch_tag + "xnb", sl)],
                                writes=[(ch_tag + "pp", pslot)])
                        sq.step("vector", lambda e, hh=hh, pslot=pslot: e.tensor_scalar(
                            kf[:], ps_p[:, pslot, 0:TC], bias_sb[:, hh:hh + 1], None, ALU.add),
                            reads=[(ch_tag + "pp", pslot), bias_tok], writes=[(ch_tag + "kf",)])
                        sq.step("scalar", lambda e: e.activation(sqb[:], kf[:], AF.Square),
                                reads=[(ch_tag + "kf",)], writes=[(ch_tag + "sq",)])
                        sq.step("tensor", lambda e: e.matmul(ps_r[0:1, 0:TC], ones_bf[:], sqb[:],
                                                             start=True, stop=True),
                                reads=[(ch_tag + "sq",), ("ones_bf",)], writes=[(ch_tag + "pr",)])
                        sq.step("vector", lambda e: e.tensor_scalar(rr[:], ps_r[0:1, 0:TC],
                                                                    1.0 / HD, float(EPS), ALU.mult, ALU.add),
                                reads=[(ch_tag + "pr",)], writes=[(ch_tag + "rr",)])
                        sq.step("scalar", lambda e: e.sqrt(rr[:], rr[:]),
                                reads=[(ch_tag + "rr",)], writes=[(ch_tag + "rr",)])
                        sq.step("vector", lambda e: e.reciprocal(rr[:], rr[:]),
                                reads=[(ch_tag + "rr",)], writes=[(ch_tag + "rr",)])
                        sq.step("tensor", lambda e: e.matmul(ps_rb[:, 0:TC], ones_f32r[:], rr[:],
                                                             start=True, stop=True),
                                reads=[(ch_tag + "rr",), ("ones_f32r",)], writes=[(ch_tag + "prb",)])
                        sq.step("vector", lambda e, t0=t0: e.tensor_tensor(
                            t1[:], kf[:], Ct[:, t0:t0 + TC], ALU.mult),
                            reads=[(ch_tag + "kf",), ("Ctr", tc)], writes=[(ch_tag + "t1",)])

                        def swap(e, t0=t0):
                            e.tensor_tensor(t2[0:64, :], kf[64:128, :], St[64:128, t0:t0 + TC], ALU.mult)
                            return e.tensor_tensor(t2[64:128, :], kf[0:64, :], St[0:64, t0:t0 + TC], ALU.mult)
                        sq.step("vector", swap, reads=[(ch_tag + "kf",), ("St_done",)],
                                writes=[(ch_tag + "t2",)])
                        sq.step("vector", lambda e: e.tensor_tensor(t1[:], t1[:], t2[:], ALU.add),
                                reads=[(ch_tag + "t1",), (ch_tag + "t2",)], writes=[(ch_tag + "t1",)])
                        obs = hh % 2
                        sq.step("vector", lambda e, obs=obs: e.tensor_tensor(
                            obuf[:, obs], t1[:], ps_rb[:, 0:TC], ALU.mult),
                            reads=[(ch_tag + "t1",), (ch_tag + "prb",)], writes=[(ch_tag + "o", obs)])
                        store(sq, "scalar", out_ap[hh * 128:(hh + 1) * 128, t0:t0 + TC],
                              obuf[:, obs], reads=[(ch_tag + "o", obs)],
                              writes=[("hbm:" + ch_tag,)], ch="st_" + ch_tag)
                sq.fence()

        qk_stage(wkT, KVD, bk_sb, ("bk_sb",), khT, KVJ, "kk")
        qk_stage(wqT, D, bq_sb, ("bq_sb",), qhT, DJ, "qq")

        with contextlib.ExitStack() as st_es:
            sb2 = lambda name, shape, dt: st_es.enter_context(nc.sbuf_tensor(name, shape, dt))
            wv_sb = sb2("wv_sb", [128, DJ, KVD], BF16)
            xnb = sb2("v_xnb", [128, 2, DJ, TC], BF16)
            vob = sb2("v_ob", [128, 2, KVD], BF16)
            ps_v = st_es.enter_context(nc.psum_tensor("ps_v", [128, 2, 512], F32))
            load_cast(sq, "gpsimd", wv_sb[:], wvT.rearrange("(j p) e -> p j e", p=128),
                      reads=[], writes=[("wv",)], ch="wv")
            load_cast(sq, "sync", bvbc[0:1, :], bvr[:], reads=[], writes=[("bvr",)], ch="bvr")
            sq.step("tensor", lambda e: e.matmul(ps_v[:, 0, 0:KVD], ones_f32r[:], bvbc[0:1, :],
                                                 start=True, stop=True),
                    reads=[("bvr",), ("ones_f32r",)], writes=[("ps_v", 0)])
            sq.step("vector", lambda e: e.tensor_copy(bvbc[:], ps_v[:, 0, 0:KVD]),
                    reads=[("ps_v", 0)], writes=[("bvbc",)])
            for tc in range(NT):
                t0 = tc * TC
                sl = tc % 2
                load_cast(sq, "sync", xnb[:, sl], _dj_view(xn1T, t0, TC),
                          reads=[("hbm:xn",)], writes=[("v_xnb", sl)], ch="v_x%d" % sl)
                for tt in range(TC // 128):
                    pslot = tt % 2

                    def mmv(e, sl=sl, tt=tt, pslot=pslot):
                        for dj in range(DJ):
                            last = e.matmul(ps_v[:, pslot, 0:KVD],
                                            xnb[:, sl, dj, tt * 128:(tt + 1) * 128],
                                            wv_sb[:, dj, :], start=(dj == 0), stop=(dj == DJ - 1))
                        return last
                    sq.step("tensor", mmv, reads=[("wv",), ("v_xnb", sl), ("bvbc",)],
                            writes=[("ps_v", pslot)])
                    sq.step("vector", lambda e, pslot=pslot: e.tensor_tensor(
                        vob[:, pslot], ps_v[:, pslot, 0:KVD], bvbc[:], ALU.add),
                        reads=[("ps_v", pslot), ("bvbc",)], writes=[("v_ob", pslot)])
                    store(sq, "scalar", V1[t0 + tt * 128:t0 + tt * 128 + 128, :], vob[:, pslot],
                          reads=[("v_ob", pslot)], writes=[("hbm:V1",)], ch="st_V1")

        sq.add_final_wait()
        with nc.Block() as block:
            sq.emit(block)
    return nc


def build_launch5(cfg):
    D, S = cfg["D"], cfg["S"]
    H1, KVH = cfg["L1_H"], cfg["L1_KVH"]
    HD = cfg["L1_HD"]
    HHq = H1 // 2
    KVJh = KVH // 2
    REP = H1 // KVH
    QC = min(512, S)
    NQC = S // QC
    KPQ = QC // 128
    ISC = 1.0 / math.sqrt(HD)
    GRP = 3
    nc = bass.Bass("TRN2", target_bir_lowering=False)
    qhTh = nc.dram_tensor("qhTh", [HHq * HD, S], BF16, kind="ExternalInput").ap()
    khTh = nc.dram_tensor("khTh", [KVJh * 128, S], BF16, kind="ExternalInput").ap()
    Vh = nc.dram_tensor("Vh", [S, KVJh * 128], BF16, kind="ExternalInput").ap()
    qgr = nc.dram_tensor("qgr", [1, HHq], F32, kind="ExternalInput").ap()
    ctxTh = nc.dram_tensor("ctxTh", [HHq * HD, S], BF16, kind="ExternalOutput").ap()

    sq = Seq(nc)
    with contextlib.ExitStack() as es:
        sb = lambda name, shape, dt: es.enter_context(nc.sbuf_tensor(name, shape, dt))
        ones_bf = sb("ones_bf", [128, 1], BF16)
        ones_f32r = sb("ones_f32r", [1, 128], F32)
        masks = sb("masks", [128, KPQ, QC], BF16)
        scl = sb("scl", [128, HHq], F32)
        qgs = sb("qgs", [1, HHq], F32)
        kths = sb("kths", [128, 2, S], BF16)
        qths = sb("qths", [128, 2, S], BF16)
        vhs = sb("vhs", [128, 2, S // 128, HD], BF16)
        Et = sb("Et", [128, 2 * GRP, 512], BF16)
        rec_row = sb("rec_row", [1, 512], F32)
        bc_sb = sb("bc_sb", [128, 512], F32)
        ob = sb("ob", [128, 2, 512], BF16)
        ps_s = es.enter_context(nc.psum_tensor("ps_s", [128, 2, GRP, 512], F32))
        ps_ctx = es.enter_context(nc.psum_tensor("ps_ctx", [128, 512], F32))
        ps_misc = es.enter_context(nc.psum_tensor("ps_misc", [128, 512], F32))

        sq.step("gpsimd", lambda e: e.memset(ones_bf[:], 1.0), writes=[("ones_bf",)])
        sq.step("gpsimd", lambda e: e.memset(ones_f32r[:], 1.0), writes=[("ones_f32r",)])
        sq.step("gpsimd", lambda e: e.memset(masks[:], 1.0), writes=[("masks",)])
        for j in range(KPQ):
            sq.step("gpsimd", lambda e, j=j: e.affine_select(
                masks[:, j], masks[:, j], [[1, QC]], ALU.is_ge, 0.0,
                base=-j * 128, channel_multiplier=-1),
                reads=[("masks",)], writes=[("masks",)])
        load_cast(sq, "sync", qgs[:], qgr[:], reads=[], writes=[("qgs",)], ch="qg")
        sq.step("tensor", lambda e: e.matmul(ps_misc[:, 0:HHq], ones_f32r[:], qgs[:],
                                             start=True, stop=True),
                reads=[("qgs",), ("ones_f32r",)], writes=[("ps_misc",)])
        sq.step("vector", lambda e: e.tensor_scalar(scl[:], ps_misc[:, 0:HHq], ISC, None, ALU.mult),
                reads=[("ps_misc",)], writes=[("scl",)])

        for h in range(HHq):
            hs = h % 2
            kv = h // REP
            load_cast(sq, "sync", kths[:, hs], khTh[kv * 128:(kv + 1) * 128, :],
                      reads=[], writes=[("kths", hs)], ch="kths%d" % hs)
            load_cast(sq, "sync", qths[:, hs], qhTh[h * HD:(h + 1) * HD, :],
                      reads=[], writes=[("qths", hs)], ch="qths%d" % hs)
            load_cast(sq, "sync", vhs[:, hs],
                      Vh.rearrange("(kt p) e -> p kt e", p=128)[:, :, kv * HD:(kv + 1) * HD],
                      reads=[], writes=[("vhs", hs)], ch="vhs%d" % hs)
            for qc in range(NQC):
                q0 = qc * QC
                n_kt = (q0 + QC) // 128
                kts = list(range(n_kt))
                groups = [kts[i:i + GRP] for i in range(0, len(kts), GRP)]
                n_av = 0

                def em_scores(gi, ts):
                    gs = gi % 2
                    n = len(ts)

                    def mm_s(e, ts=ts, gs=gs, hs=hs, q0=q0):
                        for i, kt in enumerate(ts):
                            last = e.matmul(ps_s[:, gs, i, 0:QC],
                                            kths[:, hs, kt * 128:(kt + 1) * 128],
                                            qths[:, hs, q0:q0 + QC],
                                            start=True, stop=True)
                        return last
                    sq.step("tensor", mm_s, reads=[("kths", hs), ("qths", hs)],
                            writes=[("ps_s", gs)])

                    def expg(e, n=n, gs=gs, h=h):
                        return e.activation(Et[:, gs * GRP:gs * GRP + n, 0:QC],
                                            ps_s[:, gs, 0:n, 0:QC], AF.Exp,
                                            bias=0.0, scale=scl[:, h:h + 1])
                    sq.step("scalar", expg, reads=[("ps_s", gs), ("scl",)], writes=[("Et", gs)])
                    diag = [(i, kt) for i, kt in enumerate(ts) if kt >= n_kt - KPQ]
                    if diag:
                        def maskg(e, diag=diag, gs=gs, n_kt=n_kt):
                            for i, kt in diag:
                                j = kt - (n_kt - KPQ)
                                last = e.tensor_tensor(Et[:, gs * GRP + i, 0:QC],
                                                       Et[:, gs * GRP + i, 0:QC],
                                                       masks[:, j, 0:QC], ALU.mult)
                            return last
                        sq.step("vector", maskg, reads=[("Et", gs), ("masks",)], writes=[("Et", gs)])

                def em_avsums(gi, ts, n_av):
                    gs = gi % 2

                    def av(e, ts=ts, gs=gs, hs=hs, n_av=n_av, total=n_kt):
                        for i, kt in enumerate(ts):
                            last = e.matmul(ps_ctx[:, 0:QC], vhs[:, hs, kt],
                                            Et[:, gs * GRP + i, 0:QC],
                                            start=(n_av + i == 0), stop=(n_av + i == total - 1))
                        return last
                    sq.step("tensor", av, reads=[("Et", gs), ("vhs", hs)], writes=[("ps_ctx",)])

                    def sums(e, ts=ts, gs=gs, n_av=n_av, total=n_kt):
                        for i, kt in enumerate(ts):
                            last = e.matmul(ps_misc[0:1, 0:QC], ones_bf[:],
                                            Et[:, gs * GRP + i, 0:QC],
                                            start=(n_av + i == 0), stop=(n_av + i == total - 1))
                        return last
                    sq.step("tensor", sums, reads=[("Et", gs), ("ones_bf",)], writes=[("ps_misc",)])

                for gi, ts in enumerate(groups):
                    em_scores(gi, ts)
                    if gi > 0:
                        pts = groups[gi - 1]
                        em_avsums(gi - 1, pts, n_av)
                        n_av += len(pts)
                em_avsums(len(groups) - 1, groups[-1], n_av)

                sq.step("vector", lambda e: e.reciprocal(rec_row[0:1, 0:QC], ps_misc[0:1, 0:QC]),
                        reads=[("ps_misc",)], writes=[("rec_row",)])
                sq.step("tensor", lambda e: e.matmul(ps_misc[:, 0:QC], ones_f32r[:],
                                                     rec_row[0:1, 0:QC], start=True, stop=True),
                        reads=[("rec_row",), ("ones_f32r",)], writes=[("ps_misc",)])
                sq.step("vector", lambda e: e.tensor_copy(bc_sb[:, 0:QC], ps_misc[:, 0:QC]),
                        reads=[("ps_misc",)], writes=[("bc_sb",)])
                obs = qc % 2
                sq.step("vector", lambda e, obs=obs: e.tensor_tensor(
                    ob[:, obs, 0:QC], ps_ctx[:, 0:QC], bc_sb[:, 0:QC], ALU.mult),
                    reads=[("ps_ctx",), ("bc_sb",)], writes=[("ob", obs)])
                store(sq, "scalar", ctxTh[h * HD:(h + 1) * HD, q0:q0 + QC], ob[:, obs, 0:QC],
                      reads=[("ob", obs)], writes=[("hbm:ctx",)], ch="st_ctx")

        sq.add_final_wait()
        with nc.Block() as block:
            sq.emit(block)
    return nc


def build_launch6(cfg):
    D, T = cfg["D"], cfg["T"]
    E = cfg["E"]
    I = cfg["INTER"]
    TC = min(512, T)
    DJ, NT = D // 128, T // TC
    IJ = I // 128
    nc = bass.Bass("TRN2", target_bir_lowering=False)
    x1T = nc.dram_tensor("x1T", [D, T], F32, kind="ExternalInput").ap()
    xn1T = nc.dram_tensor("xn1T", [D, T], BF16, kind="ExternalInput").ap()
    ctxT = nc.dram_tensor("ctxT", [D, T], BF16, kind="ExternalInput").ap()
    woT = nc.dram_tensor("woT", [D, D], F32, kind="ExternalInput").ap()
    boc = nc.dram_tensor("boc", [128, DJ], F32, kind="ExternalInput").ap()
    gwT = nc.dram_tensor("gwT", [D, E], F32, kind="ExternalInput").ap()
    w1T = nc.dram_tensor("w1T", [E, D, I], F32, kind="ExternalInput").ap()
    w2T = nc.dram_tensor("w2T", [E, I, D], F32, kind="ExternalInput").ap()
    w3T = nc.dram_tensor("w3T", [E, D, I], F32, kind="ExternalInput").ap()
    ws1T = nc.dram_tensor("ws1T", [D, I], F32, kind="ExternalInput").ap()
    ws2T = nc.dram_tensor("ws2T", [I, D], F32, kind="ExternalInput").ap()
    ws3T = nc.dram_tensor("ws3T", [D, I], F32, kind="ExternalInput").ap()
    outT = nc.dram_tensor("outT", [D, T], F32, kind="ExternalOutput").ap()
    attT = nc.dram_tensor("attT", [D, T], BF16, kind="Internal").ap()
    moeT = nc.dram_tensor("moeT", [D, T], F32, kind="Internal").ap()
    cTh = nc.dram_tensor("cTh", [E, T], F32, kind="Internal").ap()

    sq = Seq(nc)
    with contextlib.ExitStack() as es:
        sb = lambda name, shape, dt: es.enter_context(nc.sbuf_tensor(name, shape, dt))
        ones_bf = sb("ones_bf", [128, 1], BF16)
        ones_f32r = sb("ones_f32r", [1, 128], F32)
        ident = sb("ident", [128, 128], F32)
        cT_sb = sb("cT_sb", [E, T], F32)
        bo_sb = sb("bo_sb", [128, DJ], F32)
        sq.step("gpsimd", lambda e: e.memset(ones_bf[:], 1.0), writes=[("ones_bf",)])
        sq.step("gpsimd", lambda e: e.memset(ones_f32r[:], 1.0), writes=[("ones_f32r",)])
        sq.step("gpsimd", lambda e: e.memset(ident[:], 1.0), writes=[("ident",)])
        sq.step("gpsimd", lambda e: e.affine_select(ident[:], ident[:], [[-1, 128]],
                                                    ALU.is_equal, 0.0, base=0, channel_multiplier=1),
                reads=[("ident",)], writes=[("ident",)])
        load_cast(sq, "sync", bo_sb[:], boc[:], reads=[], writes=[("bo_sb",)], ch="bo")

        def _stageA():
          with contextlib.ExitStack() as st_es:
            sb2 = lambda name, shape, dt: st_es.enter_context(nc.sbuf_tensor(name, shape, dt))
            wo_sb = sb2("wo_sb", [128, DJ, D], BF16)
            cxb = sb2("cxb", [128, 2, DJ, TC], BF16)
            aob = sb2("aob", [128, 2, TC], BF16)
            ps_a = st_es.enter_context(nc.psum_tensor("ps_wo", [128, 2, 512], F32))
            load_cast(sq, "gpsimd", wo_sb[:], woT.rearrange("(j p) e -> p j e", p=128),
                      reads=[], writes=[("wo_sb",)], ch="wo_sb")
            for tc in range(NT):
                t0 = tc * TC
                sl = tc % 2
                load_cast(sq, "sync", cxb[:, sl], _dj_view(ctxT, t0, TC),
                          reads=[], writes=[("cxb", sl)], ch="cxb%d" % sl)
                for ec in range(DJ):
                    pslot = ec % 2

                    def mm(e, sl=sl, ec=ec, pslot=pslot):
                        for dj in range(DJ):
                            last = e.matmul(ps_a[:, pslot, 0:TC],
                                            wo_sb[:, dj, ec * 128:(ec + 1) * 128],
                                            cxb[:, sl, dj], start=(dj == 0), stop=(dj == DJ - 1))
                        return last
                    sq.step("tensor", mm, reads=[("wo_sb",), ("cxb", sl)], writes=[("ps_wo", pslot)])
                    obs = ec % 2
                    sq.step("vector", lambda e, ec=ec, pslot=pslot, obs=obs: e.tensor_scalar(
                        aob[:, obs], ps_a[:, pslot, 0:TC], bo_sb[:, ec:ec + 1], None, ALU.add),
                        reads=[("ps_wo", pslot), ("bo_sb",)], writes=[("aob", obs)])
                    store(sq, "scalar", attT[ec * 128:(ec + 1) * 128, t0:t0 + TC], aob[:, obs],
                          reads=[("aob", obs)], writes=[("hbm:att",)], ch="st_att")
            sq.fence()
        _stageA()

        def _stageB():
          with contextlib.ExitStack() as st_es:
            sb2 = lambda name, shape, dt: st_es.enter_context(nc.sbuf_tensor(name, shape, dt))
            gw_sb = sb2("gw_sb", [128, DJ, E], BF16)
            xnb = sb2("r_xnb", [128, 2, DJ, TC], BF16)
            pr = sb2("pr", [128, E], F32)
            c1 = sb2("c1", [128, 1], F32)
            c2 = sb2("c2", [128, 1], F32)
            g1 = sb2("g1", [128, E], F32)
            ps_g = st_es.enter_context(nc.psum_tensor("ps_g", [128, 512], F32))
            ps_t = st_es.enter_context(nc.psum_tensor("ps_t", [128, 512], F32))
            load_cast(sq, "gpsimd", gw_sb[:], gwT.rearrange("(j p) e -> p j e", p=128),
                      reads=[], writes=[("gw",)], ch="gw")
            for tc in range(NT):
                t0 = tc * TC
                sl = tc % 2
                load_cast(sq, "sync", xnb[:, sl], _dj_view(xn1T, t0, TC),
                          reads=[], writes=[("r_xnb", sl)], ch="r_x%d" % sl)
                for tt in range(TC // 128):
                    def mmg(e, sl=sl, tt=tt):
                        for dj in range(DJ):
                            last = e.matmul(ps_g[:, 0:E], xnb[:, sl, dj, tt * 128:(tt + 1) * 128],
                                            gw_sb[:, dj, :], start=(dj == 0), stop=(dj == DJ - 1))
                        return last
                    sq.step("tensor", mmg, reads=[("gw",), ("r_xnb", sl)], writes=[("ps_g",)])
                    sq.step("vector", lambda e: e.tensor_reduce(c1[:], ps_g[:, 0:E], AX.X,
                                                                ALU.max, negate=True),
                            reads=[("ps_g",)], writes=[("c1",)])
                    sq.step("scalar", lambda e: e.activation(pr[:], ps_g[:, 0:E], AF.Exp,
                                                             bias=c1[:], scale=1.0),
                            reads=[("ps_g",), ("c1",)], writes=[("pr",)])
                    sq.step("vector", lambda e: e.tensor_reduce(c2[:], pr[:], AX.X, ALU.add),
                            reads=[("pr",)], writes=[("c2",)])
                    sq.step("vector", lambda e: e.reciprocal(c2[:], c2[:]),
                            reads=[("c2",)], writes=[("c2",)])
                    sq.step("vector", lambda e: e.tensor_scalar(pr[:], pr[:], c2[:], None, ALU.mult),
                            reads=[("pr",), ("c2",)], writes=[("pr",)])
                    sq.step("vector", lambda e: e.tensor_reduce(c1[:], pr[:], AX.X, ALU.max),
                            reads=[("pr",)], writes=[("c1",)])
                    sq.step("vector", lambda e: e.tensor_scalar(g1[:], pr[:], c1[:], None, ALU.is_ge),
                            reads=[("pr",), ("c1",)], writes=[("g1",)])
                    sq.step("vector", lambda e: e.scalar_tensor_tensor(g1[:], g1[:], -3.0e38, pr[:],
                                                                       ALU.mult, ALU.add),
                            reads=[("g1",), ("pr",)], writes=[("g1",)])
                    sq.step("vector", lambda e: e.tensor_reduce(c2[:], g1[:], AX.X, ALU.max),
                            reads=[("g1",)], writes=[("c2",)])
                    sq.step("vector", lambda e: e.tensor_scalar(g1[:], pr[:], c2[:], None, ALU.is_ge),
                            reads=[("pr",), ("c2",)], writes=[("g1",)])
                    sq.step("vector", lambda e: e.tensor_tensor(pr[:], pr[:], g1[:], ALU.mult),
                            reads=[("pr",), ("g1",)], writes=[("pr",)])
                    sq.step("tensor", lambda e: e.transpose(ps_t[0:E, 0:128], pr[:], ident[:]),
                            reads=[("pr",), ("ident",)], writes=[("ps_t",)])
                    sq.step("vector", lambda e, t0=t0, tt=tt: e.tensor_copy(
                        cT_sb[:, t0 + tt * 128:t0 + tt * 128 + 128], ps_t[0:E, 0:128]),
                        reads=[("ps_t",)], writes=[("cT",)])
            store(sq, "scalar", cTh[:], cT_sb[:], reads=[("cT",)],
                  writes=[("hbm:cT",)], ch="st_cT")
            sq.fence()
        _stageB()

        def _expert(ex):
            shared = (ex == E)
            with contextlib.ExitStack() as st_es:
                sb2 = lambda name, shape, dt: st_es.enter_context(nc.sbuf_tensor("e%d" % ex + name, shape, dt))
                w1e = sb2("w1e", [128, DJ, I], BF16)
                w3e = sb2("w3e", [128, DJ, I], BF16)
                g_sb = sb2("g_sb", [128, IJ, T], BF16)
                xnb = sb2("xnb", [128, 2, DJ, TC], BF16)
                sil = sb2("sil", [128, TC], BF16)
                tg = sb2("tg", [128, TC], F32)
                cbc = sb2("cbc", [128, TC], F32)
                crow_e = sb2("crow_e", [1, 512], F32)
                mob = sb2("mob", [128, 2, TC], F32)
                ps_h = st_es.enter_context(nc.psum_tensor("e%d_ps_h" % ex, [128, 2, 512], F32))
                ps_u = st_es.enter_context(nc.psum_tensor("e%d_ps_u" % ex, [128, 2, 512], F32))
                ps_o = st_es.enter_context(nc.psum_tensor("e%d_ps_o" % ex, [128, 2, 512], F32))
                ps_cb = st_es.enter_context(nc.psum_tensor("e%d_ps_cb" % ex, [128, 512], F32))
                if shared:
                    w1ap, w3ap, w2ap = ws1T, ws3T, ws2T
                else:
                    w1ap, w3ap, w2ap = w1T[ex], w3T[ex], w2T[ex]
                load_cast(sq, "gpsimd", w1e[:], w1ap.rearrange("(j p) i -> p j i", p=128),
                          reads=[], writes=[("w1e",)], ch="we1")
                load_cast(sq, "gpsimd", w3e[:], w3ap.rearrange("(j p) i -> p j i", p=128),
                          reads=[], writes=[("w3e",)], ch="we3")
                for tc in range(NT):
                    t0 = tc * TC
                    sl = tc % 2
                    load_cast(sq, "sync", xnb[:, sl], _dj_view(xn1T, t0, TC),
                              reads=[], writes=[("exnb", sl)], ch="e_x%d" % sl)
                    if not shared:
                        load_cast(sq, "sync", crow_e[0:1, 0:TC], cTh[ex:ex + 1, t0:t0 + TC],
                                  reads=[("hbm:cT",)], writes=[("crow_e",)], ch="ldcrow")
                        sq.step("tensor", lambda e: e.matmul(
                            ps_cb[:, 0:TC], ones_f32r[:], crow_e[0:1, 0:TC],
                            start=True, stop=True),
                            reads=[("crow_e",), ("ones_f32r",)], writes=[("ps_cb",)])
                        sq.step("vector", lambda e: e.tensor_copy(cbc[:], ps_cb[:, 0:TC]),
                                reads=[("ps_cb",)], writes=[("cbc",)])
                    for it in range(IJ):
                        pslot = it % 2

                        def mmh(e, sl=sl, it=it, pslot=pslot):
                            for dj in range(DJ):
                                last = e.matmul(ps_h[:, pslot, 0:TC],
                                                w1e[:, dj, it * 128:(it + 1) * 128],
                                                xnb[:, sl, dj], start=(dj == 0), stop=(dj == DJ - 1))
                            return last
                        sq.step("tensor", mmh, reads=[("w1e",), ("exnb", sl)], writes=[("ps_h", pslot)])

                        def mmu(e, sl=sl, it=it, pslot=pslot):
                            for dj in range(DJ):
                                last = e.matmul(ps_u[:, pslot, 0:TC],
                                                w3e[:, dj, it * 128:(it + 1) * 128],
                                                xnb[:, sl, dj], start=(dj == 0), stop=(dj == DJ - 1))
                            return last
                        sq.step("tensor", mmu, reads=[("w3e",), ("exnb", sl)], writes=[("ps_u", pslot)])
                        sq.step("scalar", lambda e, pslot=pslot: e.activation(
                            sil[:], ps_h[:, pslot, 0:TC], AF.Silu),
                            reads=[("ps_h", pslot)], writes=[("sil",)])
                        if shared:
                            sq.step("vector", lambda e, it=it, t0=t0, pslot=pslot: e.tensor_tensor(
                                g_sb[:, it, t0:t0 + TC], ps_u[:, pslot, 0:TC], sil[:], ALU.mult),
                                reads=[("ps_u", pslot), ("sil",)], writes=[("g_sb", it, tc)])
                        else:
                            sq.step("vector", lambda e, pslot=pslot: e.tensor_tensor(
                                tg[:], ps_u[:, pslot, 0:TC], sil[:], ALU.mult),
                                reads=[("ps_u", pslot), ("sil",)], writes=[("tg",)])
                            sq.step("vector", lambda e, it=it, t0=t0: e.tensor_tensor(
                                g_sb[:, it, t0:t0 + TC], tg[:], cbc[:], ALU.mult),
                                reads=[("tg",), ("cbc",)], writes=[("g_sb", it, tc)])
                w2dst = w1e[:].rearrange("p a b -> p (a b)").rearrange(
                    "p (j d2) -> p j d2", j=IJ)
                load_cast(sq, "gpsimd", w2dst, w2ap.rearrange("(j p) d -> p j d", p=128),
                          reads=[("w1e",)], writes=[("w1e",)], ch="we1")
                for tc in range(NT):
                    t0 = tc * TC
                    for ec in range(DJ):
                        pslot = ec % 2

                        def mmo(e, ec=ec, tc=tc, t0=t0, pslot=pslot):
                            for it in range(IJ):
                                last = e.matmul(ps_o[:, pslot, 0:TC],
                                                _w2slice(w1e, IJ, it, ec),
                                                g_sb[:, it, t0:t0 + TC],
                                                start=(it == 0), stop=(it == IJ - 1))
                            return last
                        sq.step("tensor", mmo, reads=[("w1e",)] + [("g_sb", it, tc) for it in range(IJ)],
                                writes=[("ps_o", pslot)])
                        obs = ec % 2
                        sq.step("vector", lambda e, pslot=pslot, obs=obs: e.tensor_copy(
                            mob[:, obs], ps_o[:, pslot, 0:TC]),
                            reads=[("ps_o", pslot)], writes=[("mob", obs)])

                        def acc(e, mark, ec=ec, t0=t0, obs=obs, ex=ex):
                            op = ALU.bypass if ex == 0 else ALU.add
                            mark(e.dma_start(moeT[ec * 128:(ec + 1) * 128, t0:t0 + TC],
                                             mob[:, obs], accum_op=op))
                        sq.step("gpsimd", acc, reads=[("mob", obs)],
                                writes=[("hbm:moe", ec, tc)], dma=True, n_dma=1,
                                ch="moe%d" % ((ec * NT + tc) % 4))
                sq.fence()
        for ex_ in range(E + 1):
            _expert(ex_)

        def _stageD():
          with contextlib.ExitStack() as st_es:
            sb2 = lambda name, shape, dt: st_es.enter_context(nc.sbuf_tensor(name, shape, dt))
            xfb = sb2("f_x", [128, 2, TC], F32)
            afb = sb2("f_a", [128, 2, TC], BF16)
            mfb = sb2("f_m", [128, 2, TC], F32)
            t1b = sb2("f_t", [128, 2, TC], F32)
            for tc in range(NT):
                t0 = tc * TC
                for ec in range(DJ):
                    sl = ec % 2
                    load_cast(sq, "sync", xfb[:, sl], x1T[ec * 128:(ec + 1) * 128, t0:t0 + TC],
                              reads=[], writes=[("f_x", sl)], ch="f_x%d" % sl)
                    load_cast(sq, "sync", afb[:, sl], attT[ec * 128:(ec + 1) * 128, t0:t0 + TC],
                              reads=[("hbm:att",)], writes=[("f_a", sl)], ch="f_a%d" % sl)
                    load_cast(sq, "sync", mfb[:, sl], moeT[ec * 128:(ec + 1) * 128, t0:t0 + TC],
                              reads=[("hbm:moe", ec, tc)], writes=[("f_m", sl)], ch="f_m%d" % sl)
                    sq.step("vector", lambda e, sl=sl: e.tensor_tensor(
                        t1b[:, sl], xfb[:, sl], afb[:, sl], ALU.add),
                        reads=[("f_x", sl), ("f_a", sl)], writes=[("f_t", sl)])
                    sq.step("vector", lambda e, sl=sl: e.tensor_tensor(
                        t1b[:, sl], t1b[:, sl], mfb[:, sl], ALU.add),
                        reads=[("f_t", sl), ("f_m", sl)], writes=[("f_t", sl)])
                    store(sq, "scalar", outT[ec * 128:(ec + 1) * 128, t0:t0 + TC], t1b[:, sl],
                          reads=[("f_t", sl)], writes=[("hbm:out",)], ch="st_out")
        _stageD()

        sq.add_final_wait()
        with nc.Block() as block:
            sq.emit(block)
    return nc


def _w2slice(w1e, IJ, it, ec):
    flat = w1e[:].rearrange("p a b -> p (a b)")
    return flat.rearrange("p (j d) -> p j d", j=IJ)[:, it, ec * 128:(ec + 1) * 128]


def _cc(a):
    return np.ascontiguousarray(a)


def _device_run(nc, in_maps):
    res = run_bass_kernel_spmd(nc, in_maps, core_ids=list(range(len(in_maps))))
    return res.results


def kernel(**inputs):
    cfg = dict(CFG_FULL)
    cfg["T"] = cfg["S"] // 2
    return _kernel_impl(cfg, inputs, _device_run)


def _kernel_impl(cfg, inputs, run):
    B, S, D = cfg["B"], cfg["S"], cfg["D"]
    T = cfg["T"]
    HD0 = cfg["L0_HD"]
    HH0 = cfg["L0_HEADS"] // 2
    H1, KVH, HD1 = cfg["L1_H"], cfg["L1_KVH"], cfg["L1_HD"]
    HHq = H1 // 2
    KVD = KVH * HD1
    KVJh = KVH // 2
    DJ = D // 128
    KVJ = KVD // 128
    NC = 2 * B
    cores = [(c // 2, c % 2) for c in range(NC)]

    x = inputs["L0_attn_hidden_states"]
    w1m = {
        "wqT": _cc(inputs["L0_attn_wq"].T), "wkT": _cc(inputs["L0_attn_wk"].T),
        "wvT": _cc(inputs["L0_attn_wv"].T),
    }
    im = [dict(w1m, xT=_cc(x[b].T[:, hh * T:(hh + 1) * T])) for b, hh in cores]
    r1 = run(build_launch1(cfg), im)

    relT = _cc(inputs["L0_attn_rel_pos"].T)
    im = []
    for c, (b, hh) in enumerate(cores):
        rows = slice(hh * HH0 * HD0, (hh + 1) * HH0 * HD0)
        pa, pb = r1[2 * b], r1[2 * b + 1]
        im.append({
            "qTh": _cc(np.concatenate([pa["qT"][rows], pb["qT"][rows]], axis=1)),
            "kTh": _cc(np.concatenate([pa["kT"][rows], pb["kT"][rows]], axis=1)),
            "Vh": _cc(np.concatenate([pa["V"][:, rows], pb["V"][:, rows]], axis=0)),
            "relT": relT,
        })
    r2 = run(build_launch2(cfg), im)

    w3m = {
        "woT": _cc(inputs["L0_attn_wo"].T), "w1T": _cc(inputs["L0_ffn_w1"].T),
        "w2T": _cc(inputs["L0_ffn_w2"].T), "w3T": _cc(inputs["L0_ffn_w3"].T),
    }
    im = []
    for c, (b, hh) in enumerate(cores):
        ts = slice(hh * T, (hh + 1) * T)
        pa, pb = r2[2 * b], r2[2 * b + 1]
        im.append(dict(
            w3m,
            xT=_cc(x[b].T[:, ts]),
            xnT=r1[c]["xnT"],
            ctxT=_cc(np.concatenate([pa["ctxTh"][:, ts], pb["ctxTh"][:, ts]], axis=0)),
        ))
    r3 = run(build_launch3(cfg), im)

    w4m = {
        "wqT": _cc(inputs["L1_attn_Wq"].T), "wkT": _cc(inputs["L1_attn_Wk"].T),
        "wvT": _cc(inputs["L1_attn_Wv"].T),
        "bqc": _cc(inputs["L1_attn_bq"].reshape(DJ, 128).T),
        "bkc": _cc(inputs["L1_attn_bk"].reshape(KVJ, 128).T),
        "bvr": _cc(inputs["L1_attn_bv"][None, :]),
    }
    im = [dict(w4m, x1T=r3[c]["x1T"],
               trow=np.arange(hh * T, (hh + 1) * T, dtype=np.float32)[None, :])
          for c, (b, hh) in enumerate(cores)]
    r4 = run(build_launch4(cfg), im)

    qg = inputs["L1_attn_q_gain"]
    im = []
    for c, (b, hh) in enumerate(cores):
        qrows = slice(hh * HHq * HD1, (hh + 1) * HHq * HD1)
        krows = slice(hh * KVJh * 128, (hh + 1) * KVJh * 128)
        pa, pb = r4[2 * b], r4[2 * b + 1]
        im.append({
            "qhTh": _cc(np.concatenate([pa["qhT"][qrows], pb["qhT"][qrows]], axis=1)),
            "khTh": _cc(np.concatenate([pa["khT"][krows], pb["khT"][krows]], axis=1)),
            "Vh": _cc(np.concatenate([pa["V1"][:, krows], pb["V1"][:, krows]], axis=0)),
            "qgr": _cc(qg[hh * HHq:(hh + 1) * HHq][None, :]),
        })
    r5 = run(build_launch5(cfg), im)

    w6m = {
        "woT": _cc(inputs["L1_attn_Wo"].T),
        "boc": _cc(inputs["L1_attn_bo"].reshape(DJ, 128).T),
        "gwT": _cc(inputs["L1_ffn_gate_w"].T),
        "w1T": _cc(inputs["L1_ffn_w1"].transpose(0, 2, 1)),
        "w2T": _cc(inputs["L1_ffn_w2"].transpose(0, 2, 1)),
        "w3T": _cc(inputs["L1_ffn_w3"].transpose(0, 2, 1)),
        "ws1T": _cc(inputs["L1_ffn_ws1"].T), "ws2T": _cc(inputs["L1_ffn_ws2"].T),
        "ws3T": _cc(inputs["L1_ffn_ws3"].T),
    }
    im = []
    for c, (b, hh) in enumerate(cores):
        ts = slice(hh * T, (hh + 1) * T)
        pa, pb = r5[2 * b], r5[2 * b + 1]
        im.append(dict(
            w6m, x1T=r3[c]["x1T"], xn1T=r4[c]["xn1T"],
            ctxT=_cc(np.concatenate([pa["ctxTh"][:, ts], pb["ctxTh"][:, ts]], axis=0)),
        ))
    r6 = run(build_launch6(cfg), im)

    out = np.empty((B, S, D), dtype=np.float32)
    for c, (b, hh) in enumerate(cores):
        out[b, hh * T:(hh + 1) * T, :] = r6[c]["outT"].T
    return out
